# version 16
# baseline (speedup 1.0000x reference)
import numpy as np
import ml_dtypes
import concourse.bass as bass
import concourse.mybir as mybir
from concourse.bass_utils import run_bass_kernel_spmd

F32 = mybir.dt.float32
BF16 = mybir.dt.bfloat16
I32 = mybir.dt.int32
AF = mybir.ActivationFunctionType
ALU = mybir.AluOpType
AX = mybir.AxisListType

B, S, DM = 4, 4096, 2048
H, KV, HD = 8, 2, 256
SW = 512
T = 2048
HALO = 512
C = T + HALO
NT = T // 128
NCC = C // 128
HID = 8192
THETA = 1e6
NQG = 4
GQ = 512
N_CORES = 8

CONSTS = [-0.5, 0.5, 1e-6, 1e-5, -30.0, 30.0, 1.0 / 480.0]
IC_M05, IC_P05, IC_EPS6, IC_EPS5, IC_M30, IC_30, IC_1480 = range(7)

ENGS = ("sync", "vector", "scalar", "tensor", "gpsimd")


class _RecIns:
    def then_inc(self, *a, **k):
        return self


class _Rec:
    _SKIP = {"nop", "dma_start", "dma_start_transpose", "drain", "wait_ge"}

    def __init__(self):
        self.r, self.w = [], []
        self.conservative = False

    def __getattr__(self, name):
        def h(*a, **k):
            if name in _Rec._SKIP:
                return _RecIns()
            if "accum_out" in k and k["accum_out"] is not None:
                self.conservative = True
            aps = [x for x in a if hasattr(x, "tensor")]
            if aps:
                self.w.append(aps[0])
                self.r += aps[1:]
            for kk, v in k.items():
                if hasattr(v, "tensor"):
                    (self.w if kk in ("accum_out", "out") else self.r).append(v)
            return _RecIns()
        return h


class Prog:

    CROSS = {"init_ld", "init_v", "w1m_done", "f_s3", "w2q_st", "bar",
             "b_wld", "w1_ld0", "w1_ld1", "c_wld", "w2_ld0", "w2_ld1",
             "w2_ld20", "w2_ld21", "f_w1ld0", "f_w1ld1", "g_w2ld", "g_acst"} | {"g_ac%02d" % i for i in range(16)}

    def __init__(self, nc):
        self.nc = nc
        self.ops = {e: [] for e in ENGS}
        self.sems = {}
        self.cnt = {}
        self._local = {}
        self.guar = {}
        self.acc_w = {}
        self.acc_r = {}

    @staticmethod
    def _ivs(aps):
        out = []
        for ap in aps:
            try:
                reach = 1
                for s, n in ap.ap[1:]:
                    reach += abs(s) * (n - 1)
                off = ap.offset
                out.append((ap.tensor.name, off, off + reach))
            except Exception:
                return None
        return out

    @staticmethod
    def _clash(ws, rw):
        if ws is None or rw is None:
            return True
        for an, al, ah in ws:
            for bn, bl, bh in rw:
                if an == bn and al < bh and bl < ah:
                    return True
        return False

    def new_phase(self):
        self._local = {}

    def alias(self, name):
        if name.startswith("pool") or name == "bar":
            return name
        if name.startswith("bar_"):
            return "bar"
        if name in self.CROSS:
            return name
        if name not in self._local:
            self._local[name] = "pool%02d" % len(self._local)
        return self._local[name]

    def sem(self, name):
        name = self.alias(name)
        if name not in self.sems:
            self.sems[name] = self.nc.semaphore(name).__enter__()
            self.cnt[name] = 0
        return self.sems[name]

    def emit(self, eng, fn, waits=(), inc=None, inc_n=1, nodrain=False,
             sem_ok=False):
        g = self.guar.setdefault(eng, {})
        ws = []
        for s, v in waits:
            s = self.alias(s)
            if g.get(s, -1) >= v:
                continue
            g[s] = v
            ws.append((s, v))
            self.sem(s)
        if eng in ("vector", "scalar") and fn is not None:
            if nodrain:
                cur_r, cur_w = [], []
            else:
                try:
                    rec = _Rec()
                    fn(rec)
                    if rec.conservative:
                        cur_w = cur_r = None
                    else:
                        cur_w = self._ivs(rec.w)
                        cur_r = self._ivs(rec.r)
                except Exception:
                    cur_w = cur_r = None
            acc_w = self.acc_w.get(eng, [])
            acc_r = self.acc_r.get(eng, [])
            if cur_w is None or cur_r is None:
                cur_rw = None
            else:
                cur_rw = cur_w + cur_r
            drain_before = (not sem_ok and
                            (self._clash(acc_w, cur_rw) or
                             self._clash(acc_r, cur_w)))
            if drain_before:
                self.acc_w[eng] = cur_w
                self.acc_r[eng] = cur_r
            else:
                self.acc_w[eng] = (None if (acc_w is None or cur_w is None)
                                   else acc_w + cur_w)
                self.acc_r[eng] = (None if (acc_r is None or cur_r is None)
                                   else acc_r + cur_r)
        else:
            drain_before = False
        entry = [fn, ws, None, drain_before]
        if inc is not None:
            inc = self.alias(inc)
            self.sem(inc)
            self.cnt[inc] += inc_n
            entry[2] = (inc, inc_n)
        self.ops[eng].append(entry)
        return self.cnt[inc] if inc is not None else None

    def val(self, sem_name):
        self.sem(sem_name)
        return self.cnt[self.alias(sem_name)]

    def dma(self, eng, out, in_, waits=(), inc=None):
        return self.emit(eng, lambda e, o=out, i=in_: e.dma_start(out=o, in_=i),
                         waits, inc, 16, nodrain=True)

    def dma_t(self, eng, out, in_, waits=(), inc=None):
        return self.emit(eng, lambda e, o=out, i=in_: e.dma_start_transpose(o, i),
                         waits, inc, 16, nodrain=True)

    def barrier(self, tag):
        s = "bar"
        self.sem(s)
        flush = [(n, v) for n, v in self.cnt.items()
                 if v > 0 and not n.startswith("bar_")]
        for n, v in flush:
            self.emit("sync", None, waits=[(n, v)])
        tgt = self.cnt[s] + len(ENGS)
        for e in ENGS:
            self.emit(e, lambda eng: eng.nop(), inc=s)
        for e in ENGS:
            self.emit(e, None, waits=[(s, tgt)])
        return tgt

    def replay(self, block):
        def mk(name):
            def body(eng):
                for fn, waits, inc, drain_before in self.ops[name]:
                    for s, v in waits:
                        eng.wait_ge(self.sems[s], v)
                    if fn is None:
                        continue
                    if drain_before:
                        eng.drain()
                    ins = fn(eng)
                    if inc is not None:
                        ins.then_inc(self.sems[inc[0]], inc[1])
            return body
        block.sync(mk("sync"))
        block.vector(mk("vector"))
        block.scalar(mk("scalar"))
        block.tensor(mk("tensor"))
        block.gpsimd(mk("gpsimd"))


class Arena:

    def __init__(self, nc, stack, name, dtype, cols):
        self.t = stack.enter_context(nc.sbuf_tensor(name, [128, cols], dtype))
        self.cols = cols
        self.off = 0

    def reset(self):
        self.off = 0

    def take(self, cols):
        assert self.off + cols <= self.cols, \
            f"arena {self.t.name} overflow: {self.off}+{cols}>{self.cols}"
        a = self.t[:, self.off:self.off + cols]
        self.off += cols
        return a


def build_program(dev_taps=False, phases=("1a", "1b", "1c", "2p", "2", "3")):
    nc = bass.Bass("TRN2", target_bir_lowering=False)
    skind = "ExternalOutput" if dev_taps else "Internal"

    x_tm = nc.dram_tensor("x_tm", [C, DM], F32, kind="ExternalInput").ap()
    wqT = nc.dram_tensor("wqT", [DM, DM], F32, kind="ExternalInput").ap()
    wkT = nc.dram_tensor("wkT", [DM, KV * HD], F32, kind="ExternalInput").ap()
    wvT = nc.dram_tensor("wvT", [DM, KV * HD], F32, kind="ExternalInput").ap()
    woT = nc.dram_tensor("woT", [DM, DM], F32, kind="ExternalInput").ap()
    w1T = nc.dram_tensor("w1T", [DM, HID], F32, kind="ExternalInput").ap()
    w2T = nc.dram_tensor("w2T", [HID, DM], F32, kind="ExternalInput").ap()
    ropec = nc.dram_tensor("ropec", [C, 128], F32, kind="ExternalInput").ap()
    ropes = nc.dram_tensor("ropes", [C, 128], F32, kind="ExternalInput").ap()
    masks = nc.dram_tensor("masks", [16, 128, GQ], BF16, kind="ExternalInput").ap()
    onesb = nc.dram_tensor("onesb", [128, 128], BF16, kind="ExternalInput").ap()
    consts_in = nc.dram_tensor("consts", [128, len(CONSTS)], F32,
                               kind="ExternalInput").ap()

    y_out = nc.dram_tensor("y", [T, DM], F32, kind="ExternalOutput").ap()

    xn_hbm = nc.dram_tensor("xn_hbm", [C, DM], BF16, kind=skind).ap()
    q_hbm = nc.dram_tensor("q_hbm", [T, DM], BF16, kind=skind).ap()
    k_hbm = nc.dram_tensor("k_hbm", [C, KV * HD], BF16, kind=skind).ap()
    v_hbm = nc.dram_tensor("v_hbm", [C, KV * HD], BF16, kind=skind).ap()
    x1_hbm = nc.dram_tensor("x1_hbm", [T, DM], F32, kind=skind).ap()
    xq_hbm = nc.dram_tensor("xq_hbm", [T, DM], BF16, kind=skind).ap()
    h_hbm = nc.dram_tensor("h_hbm", [T, HID], F32, kind=skind).ap()
    hq_hbm = nc.dram_tensor("hq_hbm", [T, HID], BF16, kind=skind).ap()
    w2q_hbm = nc.dram_tensor("w2q_hbm", [HID, DM], BF16, kind=skind).ap()
    yacc_hbm = nc.dram_tensor("yacc_hbm", [T, DM], F32, kind=skind).ap()

    P = Prog(nc)
    import contextlib
    top = contextlib.ExitStack()
    with top:
        consts = top.enter_context(nc.sbuf_tensor("consts_sb", [128, len(CONSTS)], F32))
        ones_sb = top.enter_context(nc.sbuf_tensor("ones_sb", [128, 128], BF16))
        onesf_sb = top.enter_context(nc.sbuf_tensor("onesf_sb", [128, 128], F32))
        amax2c = top.enter_context(nc.sbuf_tensor("amax2c", [128, NT], F32))
        s2col = top.enter_context(nc.sbuf_tensor("s2col", [128, NT], F32))
        amax3r = top.enter_context(nc.sbuf_tensor("amax3r", [128, NT], F32))
        amax3c = top.enter_context(nc.sbuf_tensor("amax3c", [128, NT], F32))
        s3col = top.enter_context(nc.sbuf_tensor("s3col", [128, NT], F32))
        fac2 = top.enter_context(nc.sbuf_tensor("fac2", [128, NT], F32))
        fac3 = top.enter_context(nc.sbuf_tensor("fac3", [128, NT], F32))
        w1m_col = top.enter_context(nc.sbuf_tensor("w1m_col", [128, 1], F32))
        w2m_col = top.enter_context(nc.sbuf_tensor("w2m_col", [128, 1], F32))
        w1inv = top.enter_context(nc.sbuf_tensor("w1inv", [128, 1], F32))
        w2inv = top.enter_context(nc.sbuf_tensor("w2inv", [128, 1], F32))
        wsum1 = top.enter_context(nc.sbuf_tensor("wsum1", [128, 16], F32))
        wsum2 = top.enter_context(nc.sbuf_tensor("wsum2", [128, 64], F32))
        bc_sc = top.enter_context(nc.sbuf_tensor("bc_sc", [128, 2], F32))
        A16 = Arena(nc, top, "arena_bf16", BF16, 77824)
        A32 = Arena(nc, top, "arena_f32", F32, 12544)
        AI = Arena(nc, top, "arena_i32", I32, 1024)

        def phase_reset():
            A16.reset(); A32.reset(); AI.reset()
            P.new_phase()

        cc_ = lambda i: consts[:, i:i + 1]

        P.dma("sync", consts[:], consts_in[:, :], inc="init_ld")
        P.dma("sync", ones_sb[:], onesb[:, :], inc="init_ld")
        init_done = P.val("init_ld")
        P.emit("vector", lambda e: e.tensor_copy(onesf_sb[:], ones_sb[:]),
               waits=[("init_ld", init_done)])
        P.emit("scalar", None, waits=[("init_ld", init_done)])

        def ln_stats(x_ap, W, st, sq_small, eps_col, tag, waits_first, inv_w,
                     defer_var=None):
            P.emit("vector", lambda e: e.reduce_sum(
                st[:, 0:1].unsqueeze(2),
                x_ap.unsqueeze(1), AX.X), waits=waits_first)
            hwp = W // 2
            P.emit("scalar", lambda e: e.activation(
                sq_small[:, :hwp], x_ap[:, 0:hwp], AF.Square,
                accum_out=st[:, 8:9]),
                waits=list(waits_first) + [(tag + "sqf", P.val(tag + "sqf"))])
            sqd = P.emit("scalar", lambda e: e.activation(
                sq_small[:, :hwp], x_ap[:, hwp:W], AF.Square,
                accum_out=st[:, 9:10]), inc=tag + "sq")
            P.emit("vector", lambda e: e.tensor_tensor(
                st[:, 1:2], st[:, 8:9], st[:, 9:10], ALU.add),
                waits=[(tag + "sq", sqd)], inc=tag + "sqf")
            P.emit("vector", lambda e: e.tensor_scalar(
                st[:, 2:4], st[:, 0:2], inv_w, None, ALU.mult))
            P.emit("vector", lambda e: e.tensor_tensor(
                st[:, 4:5], st[:, 2:3], st[:, 2:3], ALU.mult),
                waits=([(tag + "sd", P.val(tag + "sd"))]
                       if defer_var is None else []))
            if defer_var is not None:
                P.emit("vector", lambda e: e.tensor_tensor(
                    defer_var, st[:, 3:4], st[:, 4:5], ALU.subtract),
                    inc=tag + "v")
                return
            vd = P.emit("vector", lambda e: e.tensor_tensor(
                st[:, 5:6], st[:, 3:4], st[:, 4:5], ALU.subtract), inc=tag + "v")
            sdd = P.emit("scalar", lambda e: e.activation(
                st[:, 6:7], st[:, 5:6], AF.Sqrt, bias=eps_col),
                waits=[(tag + "v", vd)], inc=tag + "sd")
            P.emit("vector", lambda e: e.reciprocal(st[:, 7:8], st[:, 6:7]),
                   waits=[(tag + "sd", sdd)])

        def _ph_1a():
            xbuf = [A32.take(DM) for i in range(2)]
            sqsm = A32.take(DM // 2)
            xnb = [A16.take(DM) for i in range(2)]
            stat = A32.take(16)

            def emit_chunk(cc):
                b = cc % 2
                w = [("a_free", P.val("a_free"))] if cc >= 2 else []
                ld = P.dma("sync", xbuf[b][:], x_tm[cc * 128:(cc + 1) * 128, :],
                           waits=w, inc="a_ld")
                ln_stats(xbuf[b][:], DM, stat, sqsm, cc_(IC_EPS5), "a",
                         [("a_ld", ld)], 1.0 / DM)
                wapp = []
                if cc >= 2:
                    sprev = "a_st%d" % ((cc - 2) // 4)
                    wapp = [(sprev, P.val(sprev))]
                apd = P.emit("vector", lambda e, b=b: e.tensor_scalar(
                    xnb[b][:], xbuf[b][:], stat[:, 2:3], stat[:, 7:8],
                    ALU.subtract, ALU.mult), waits=wapp, inc="a_ap")
                P.emit("vector", lambda e: e.nop(), nodrain=True,
                       waits=[("a_ap", apd)], inc="a_free")
                P.dma("sync", xn_hbm[cc * 128:(cc + 1) * 128, :], xnb[b][:],
                      waits=[("a_ap", apd)], inc="a_st%d" % (cc // 4))
            return emit_chunk

        def _ph_1b():
            NG, GCH = 5, 4
            GTOK = GCH * 128
            with contextlib.ExitStack() as ph:
                phase_reset()
                wq_sb = A16.take(16 * DM)
                wk_sb = A16.take(16 * KV * HD)
                wv_sb = A16.take(16 * KV * HD)
                xnf = [A16.take(16 * GTOK)]
                cosb = [A32.take(GTOK) for i in range(2)]
                sinb = [A32.take(GTOK) for i in range(2)]
                qf = [A16.take(DM)]
                kf = [A16.take(KV * HD)]
                t1q = A16.take(DM + 2 * KV * HD)
                t2q = A16.take(DM)
                sqsm = A32.take(HD)
                qkvp = A16.take(DM + 2 * KV * HD)
                qkvo = A16.take(DM + 2 * KV * HD)
                stq = A32.take(64)
                w1p = [A32.take(DM) for i in range(2)]
                ps_q = ph.enter_context(nc.psum_tensor("ps_q", [128, DM], F32))
                ps_bc = ps_q
                ps_k = [ph.enter_context(nc.psum_tensor(f"ps_k{i}", [128, KV * HD], F32)) for i in range(2)]
                ps_v = [ph.enter_context(nc.psum_tensor(f"ps_v{i}", [128, KV * HD], F32)) for i in range(2)]

                for kc in range(16):
                    P.dma("gpsimd", wq_sb[:, kc * DM:(kc + 1) * DM],
                          wqT[kc * 128:(kc + 1) * 128, :], inc="b_wld")
                    P.dma("gpsimd", wk_sb[:, kc * KV * HD:(kc + 1) * KV * HD],
                          wkT[kc * 128:(kc + 1) * 128, :], inc="b_wld")
                    P.dma("gpsimd", wv_sb[:, kc * KV * HD:(kc + 1) * KV * HD],
                          wvT[kc * 128:(kc + 1) * 128, :], inc="b_wld")
                b_wld_all = P.val("b_wld")

                def w1_mean_slice(i):
                    rc, qt = i // 4, i % 4
                    pb = i % 2
                    w = [("w1ab_done", max(0, P.val("w1ab_done") - 1))] if i >= 2 else []
                    ld = P.dma("gpsimd", w1p[pb][:],
                               w1T[rc * 128:(rc + 1) * 128, qt * DM:(qt + 1) * DM],
                               waits=w, inc="w1_ld%d" % pb)
                    P.emit("scalar", lambda e, pb=pb, i=i: e.activation(
                        w1p[pb][:], w1p[pb][:], AF.Abs,
                        accum_out=wsum2[:, i:i + 1]), waits=[("w1_ld%d" % pb, ld)],
                        inc="w1ab_done")

                def rope(src, width, nheads, gb, tl, out_bf, tag, w_first, freetag):
                    hw = 128
                    cs1 = cosb[gb][:, tl * 128:(tl + 1) * 128].unsqueeze(1).broadcast_to([128, nheads, hw])
                    sn1 = sinb[gb][:, tl * 128:(tl + 1) * 128].unsqueeze(1).broadcast_to([128, nheads, hw])
                    cs2 = cosb[gb][:, tl * 128:(tl + 1) * 128].unsqueeze(1).unsqueeze(1).broadcast_to([128, nheads, 2, hw])
                    x4 = src.rearrange("p (h two j) -> p h two j", two=2, j=hw)
                    t14 = t1q[:, :width].rearrange("p (h two j) -> p h two j", two=2, j=hw)
                    t24 = t2q[:, :width].rearrange("p (h two j) -> p h two j", two=2, j=hw)
                    o4 = out_bf.rearrange("p (h two j) -> p h two j", two=2, j=hw)
                    P.emit("vector", lambda e: e.tensor_tensor(t14[:], x4[:], cs2, ALU.mult),
                           waits=w_first)
                    P.emit("vector", lambda e: e.tensor_tensor(
                        t24[:, :, 0, :], x4[:, :, 1, :], sn1, ALU.mult))
                    P.emit("vector", lambda e: e.tensor_tensor(
                        t24[:, :, 1, :], x4[:, :, 0, :], sn1, ALU.mult),
                        inc=freetag)
                    P.emit("vector", lambda e: e.tensor_tensor(
                        o4[:, :, 0, :], t14[:, :, 0, :], t24[:, :, 0, :], ALU.subtract))
                    return P.emit("vector", lambda e: e.tensor_tensor(
                        o4[:, :, 1, :], t14[:, :, 1, :], t24[:, :, 1, :], ALU.add),
                        inc=tag + "rp")

                def post_ln(x_bf, off, nseg, q_heads, w_first, apply_waits):
                    tag = "bm"
                    xv = x_bf[:, off:off + nseg * HD]
                    x3 = xv.rearrange("p (h d) -> p h d", d=HD)
                    s_sum = stq[:, 0:nseg]
                    s_ssq = stq[:, 16:16 + nseg]
                    s_mu = stq[:, 32:32 + nseg]
                    s_var = stq[:, 48:48 + nseg]
                    P.emit("vector", lambda e: e.tensor_reduce(
                        s_sum.unsqueeze(2), x3[:], AX.X, ALU.add),
                        waits=list(w_first) + [(tag + "sd", P.val(tag + "sd"))])
                    sqd = None
                    for hh in range(nseg):
                        sqd = P.emit("scalar", lambda e, hh=hh: e.activation(
                            sqsm[:], xv[:, hh * HD:(hh + 1) * HD], AF.Square,
                            accum_out=s_ssq[:, hh:hh + 1]),
                            waits=(list(w_first) + [(tag + "v", P.val(tag + "v"))]
                                   if hh == 0 else []),
                            inc=tag + "sq")
                    P.emit("vector", lambda e: e.tensor_scalar(
                        s_mu, s_sum, 1.0 / HD, None, ALU.mult),
                        waits=[(tag + "sq", sqd)])
                    P.emit("vector", lambda e: e.tensor_scalar(
                        s_var, s_ssq, 1.0 / HD, None, ALU.mult))
                    P.emit("vector", lambda e: e.scalar_tensor_tensor(
                        s_sum, s_mu, 1.0, s_mu, ALU.mult, ALU.mult))
                    vd = P.emit("vector", lambda e: e.tensor_tensor(
                        s_var, s_var, s_sum, ALU.subtract), inc=tag + "v")
                    sdd = P.emit("scalar", lambda e: e.activation(
                        s_sum, s_var, AF.Sqrt, bias=cc_(IC_EPS6)),
                        waits=[(tag + "v", vd)], inc=tag + "sd")
                    P.emit("vector", lambda e: e.reciprocal(s_var, s_sum),
                           waits=[(tag + "sd", sdd)])
                    if q_heads:
                        P.emit("vector", lambda e: e.tensor_scalar(
                            s_var[:, 0:q_heads], s_var[:, 0:q_heads], 256.0, None, ALU.mult))
                    P.emit("vector", lambda e: e.tensor_tensor(
                        s_mu, s_mu, s_var, ALU.mult))
                    rs_b = s_var.unsqueeze(2).broadcast_to([128, nseg, HD])
                    b_b = s_mu.unsqueeze(2).broadcast_to([128, nseg, HD])
                    ov = qkvo[:, off:off + nseg * HD].rearrange("p (h d) -> p h d", d=HD)
                    t3 = t1q[:, :nseg * HD].rearrange("p (h d) -> p h d", d=HD)
                    P.emit("vector", lambda e: e.tensor_tensor(t3[:], x3[:], rs_b, ALU.mult),
                           waits=apply_waits)
                    return P.emit("vector", lambda e: e.tensor_tensor(
                        ov[:], t3[:], b_b, ALU.subtract), inc=tag + "ap")

                emit_1a_chunk = _ph_1a()
                for cc4 in range(4):
                    emit_1a_chunk(cc4)
                for g in range(NG):
                    if g + 1 < NG:
                        for cc4 in range(4):
                            emit_1a_chunk((g + 1) * GCH + cc4)
                    gb = 0
                    wld = [("b_gfree", P.val("b_gfree"))] if g >= 1 else []
                    rb2 = g % 2
                    P.dma("sync", cosb[rb2].rearrange("p (c j) -> p c j", j=128),
                          ropec[g * GTOK:(g + 1) * GTOK, :].rearrange("(c p) j -> p c j", p=128),
                          waits=(wld + ([("bqap", P.val("bqap"))] if g >= 2 else [])),
                          inc="b_rld")
                    P.dma("sync", sinb[rb2].rearrange("p (c j) -> p c j", j=128),
                          ropes[g * GTOK:(g + 1) * GTOK, :].rearrange("(c p) j -> p c j", p=128),
                          inc="b_rld")
                    for kc in range(16):
                        P.dma_t("sync", xnf[gb][:, kc * GTOK:(kc + 1) * GTOK],
                                xn_hbm[g * GTOK:(g + 1) * GTOK, kc * 128:(kc + 1) * 128],
                                waits=([("a_st%d" % g, 64)] if kc == 0 else []),
                                inc="b_xn")
                    b_xn_need = P.val("b_xn")
                    b_rld_need = P.val("b_rld")
                    for tl in range(GCH):
                        cc = g * GCH + tl
                        for k4 in range(4):
                            i_w1 = cc * 4 + k4
                            if i_w1 < 64:
                                w1_mean_slice(i_w1)
                        own = cc >= 4
                        tco = cc - 4
                        ob = cc % 2

                        wl = [("b_wld", b_wld_all), ("b_xn", b_xn_need),
                              ("b_rld", b_rld_need)]
                        if P.val("b_kev") >= 2:
                            wl += [("b_kev", P.val("b_kev") - 1),
                                   ("b_vev", P.val("b_vev") - 1)]
                        wq_mm = [("b_qev", P.val("b_qev"))] if P.val("b_qev") > 0 else []
                        mms = []
                        for kc in range(16):
                            lhs = xnf[gb][:, kc * GTOK + tl * 128: kc * GTOK + (tl + 1) * 128]
                            mms.append((ps_k[ob][:, :], lhs,
                                        wk_sb[:, kc * 512:(kc + 1) * 512], kc == 0, kc == 15, None))
                        for kc in range(16):
                            lhs = xnf[gb][:, kc * GTOK + tl * 128: kc * GTOK + (tl + 1) * 128]
                            mms.append((ps_v[ob][:, :], lhs,
                                        wv_sb[:, kc * 512:(kc + 1) * 512], kc == 0, kc == 15, None))
                        if own:
                            firstq = True
                            for kc in range(16):
                                lhs = xnf[gb][:, kc * GTOK + tl * 128: kc * GTOK + (tl + 1) * 128]
                                for nb in range(4):
                                    mms.append((ps_q[:, nb * 512:(nb + 1) * 512], lhs,
                                                wq_sb[:, kc * DM + nb * 512: kc * DM + (nb + 1) * 512],
                                                kc == 0, kc == 15,
                                                (wq_mm if firstq else None)))
                                    firstq = False
                        for n_, (o, l, r, st_, sp, xw) in enumerate(mms):
                            ww = list(wl) if n_ == 0 else []
                            if xw:
                                ww += xw
                            mmd = P.emit("tensor", lambda e, o=o, l=l, r=r, st_=st_, sp=sp:
                                         e.matmul(o, l, r, start=st_, stop=sp),
                                         waits=ww,
                                         inc=("b_mm" if n_ == len(mms) - 1 else None))

                        wev = [("b_mm", mmd)]
                        if cc >= 1:
                            wev += [("bkfree", P.val("bkfree")),
                                    ("bqfree", P.val("bqfree")),
                                    ("bmap", P.val("bmap"))]
                        kev = P.emit("scalar", lambda e, ob=ob: e.copy(kf[0][:], ps_k[ob][:]),
                                     waits=wev, inc="b_kev")
                        vev = P.emit("scalar", lambda e, ob=ob: e.copy(
                            qkvp[:, DM + KV * HD:], ps_v[ob][:]), inc="b_vev")
                        wnop = [("b_kev", kev), ("b_vev", vev)]
                        if own:
                            qev = P.emit("scalar", lambda e: e.copy(qf[0][:], ps_q[:]),
                                         inc="b_qev")
                            wnop.append(("b_qev", qev))
                        P.emit("scalar", lambda e: e.nop(), nodrain=True,
                               waits=wnop, inc="b_ev")

                        wo = [("b_ost", P.val("b_ost"))] if cc >= 1 else []
                        rpk = rope(kf[0][:], KV * HD, KV, rb2, tl,
                                   qkvp[:, DM:DM + KV * HD], "bk",
                                   [("b_kev", kev)], "bkfree")
                        if own:
                            rpq = rope(qf[0][:], DM, H, rb2, tl, qkvp[:, 0:DM],
                                       "bq", [("b_qev", qev)], "bqfree")
                            lw = [("bkrp", rpk), ("bqrp", rpq), ("b_vev", vev)]
                            ap = post_ln(qkvp, 0, 12, 8, lw, wo)
                        else:
                            P.emit("vector", lambda e: e.nop(), nodrain=True, inc="bqfree")
                            lw = [("bkrp", rpk), ("b_vev", vev)]
                            ap = post_ln(qkvp, DM, 4, 0, lw, wo)
                        P.emit("vector", lambda e: e.nop(), nodrain=True,
                               waits=[("bmap", ap)], inc="bmap")

                        P.dma("sync", k_hbm[cc * 128:(cc + 1) * 128, :],
                              qkvo[:, DM:DM + KV * HD],
                              waits=[("bmap", P.val("bmap"))], inc="b_ost")
                        P.dma("sync", v_hbm[cc * 128:(cc + 1) * 128, :],
                              qkvo[:, DM + KV * HD:], inc="b_ost")
                        if own:
                            P.dma("sync", q_hbm[tco * 128:(tco + 1) * 128, :],
                                  qkvo[:, 0:DM], inc="b_ost")
                    P.emit("tensor", lambda e: e.nop(),
                           waits=[("b_mm", P.val("b_mm"))], inc="b_gfree")

                P.emit("vector", lambda e: e.reduce_sum(
                    bc_sc[:, 0:1].unsqueeze(2),
                    wsum2[:, 0:64].unsqueeze(1), AX.X),
                    waits=[("w1ab_done", P.val("w1ab_done"))], inc="w1s")
                m1 = P.emit("tensor", lambda e: e.matmul(
                    ps_bc[0:1, 0:1], onesf_sb[:, 0:1], bc_sc[:, 0:1],
                    start=True, stop=True),
                    waits=[("w1s", P.val("w1s")), ("b_qev", P.val("b_qev"))],
                    inc="w1s_mm")
                e1 = P.emit("scalar", lambda e: e.copy(bc_sc[0:1, 1:2], ps_bc[0:1, 0:1]),
                            waits=[("w1s_mm", m1)], inc="w1s_ev")
                m2 = P.emit("tensor", lambda e: e.matmul(
                    ps_bc[:, 0:1], onesf_sb[0:1, :], bc_sc[0:1, 1:2],
                    start=True, stop=True), waits=[("w1s_ev", e1)], inc="w1s_mm")
                e2 = P.emit("scalar", lambda e: e.copy(w1m_col[:], ps_bc[:, 0:1]),
                            waits=[("w1s_mm", m2)], inc="w1s_ev")
                P.emit("vector", lambda e: e.tensor_scalar(
                    w1m_col[:], w1m_col[:], 1.0 / (DM * HID), 1e-5, ALU.mult, ALU.max),
                    waits=[("w1s_ev", e2)])
                P.emit("vector", lambda e: e.reciprocal(w1inv[:], w1m_col[:]),
                       inc="w1m_done")
            P.barrier("p1b")

        def _ph_1c():
            with contextlib.ExitStack() as ph:
                phase_reset()
                wo_sb = A16.take(16 * DM)
                mask_sb = A16.take(8 * GQ)
                qfm = [A16.take(16 * GQ) for i in range(2)]
                kfm = A16.take(4 * 1024)
                vtm = A16.take(8 * 512)
                pt_sb = A16.take(8 * GQ)
                th_s = A32.take(2 * GQ)
                u_s = A32.take(2 * GQ)
                z_a = A32.take(GQ)
                varc = A32.take(16)
                tmpc2 = A32.take(16)
                afm = A16.take(16 * GQ)
                xres = [A32.take(DM)]
                x1t = A32.take(DM)
                xn2t = A32.take(DM)
                sqsm2 = A32.take(DM // 2)
                scrA = A32.take(DM // 2)
                scrB = A32.take(DM // 2)
                zi32 = AI.take(DM // 2)
                xqv = A16.take(DM)
                st2 = A32.take(16)
                tmpc = A32.take(2)
                ps_st = [ph.enter_context(nc.psum_tensor(f"ps_st{i}", [128, 2 * GQ], F32)) for i in range(2)]
                ps_z = ph.enter_context(nc.psum_tensor("ps_z", [128, GQ], F32))
                ps_pv = [ph.enter_context(nc.psum_tensor(f"ps_pv{i}", [128, GQ], F32)) for i in range(2)]
                ps_o = ph.enter_context(nc.psum_tensor("ps_o", [128, 512], F32))

                P.dma("sync", mask_sb.rearrange("p (t q) -> p t q", q=GQ),
                      masks[8:16, :, :].rearrange("t p q -> p t q"), inc="c_mld")
                for kc in range(16):
                    P.dma("gpsimd", wo_sb[:, kc * DM:(kc + 1) * DM],
                          woT[kc * 128:(kc + 1) * 128, :], inc="c_wld")
                c_wld_all = P.val("c_wld")
                c_mld_all = P.val("c_mld")

                qfm_free = [0, 0]
                pend_tasks = []

                def ln2_tasks(tc, x1st_need):
                    def t0():
                        wld = [("c_x1st", x1st_need)]
                        if P.val("c_q") > 0:
                            wld.append(("c_q", P.val("c_q")))
                        ld = P.dma(
                            "sync", xn2t[:], x1_hbm[tc * 128:(tc + 1) * 128, :],
                            waits=wld, inc="c_xrld")
                        ln_stats(xn2t[:], DM, st2, sqsm2, cc_(IC_EPS5), "p",
                                 [("c_xrld", ld)], 1.0 / DM,
                                 defer_var=varc[:, tc:tc + 1])

                    def t1():
                        P.emit("vector", lambda e: e.tensor_scalar(
                            xn2t[:], xn2t[:], st2[:, 2:3], None,
                            ALU.subtract), inc="c_x1done")
                        P.emit("vector", lambda e: e.tensor_reduce(
                            amax2c[:, tc:tc + 1].unsqueeze(2),
                            xn2t[:, :].unsqueeze(1), AX.X, ALU.max,
                            apply_absolute_value=True))
                        P.emit("vector", lambda e: e.reciprocal(
                            tmpc[:, 0:1], amax2c[:, tc:tc + 1]))
                        P.emit("vector", lambda e: e.tensor_scalar(
                            s2col[:, tc:tc + 1], tmpc[:, 0:1], 127.0, None, ALU.mult))

                    def mk_q(pc):
                        def tq():
                            cs_ = slice(pc * (DM // 2), (pc + 1) * (DM // 2))
                            wq_ = ([("c_xqst", P.val("c_xqst"))]
                                   if (tc >= 1 and pc == 0) else [])
                            P.emit("vector", lambda e: e.tensor_scalar(
                                scrA[:], xn2t[:, cs_], s2col[:, tc:tc + 1],
                                None, ALU.mult), waits=wq_)
                            P.emit("vector", lambda e: e.tensor_copy(zi32[:], scrA[:]))
                            P.emit("vector", lambda e: e.scalar_tensor_tensor(
                                scrB[:], scrA[:], 0.5, zi32[:], ALU.subtract, ALU.is_ge))
                            P.emit("vector", lambda e: e.scalar_tensor_tensor(
                                xn2t[:, cs_], scrA[:], 0.5, zi32[:], ALU.add, ALU.is_le))
                            P.emit("vector", lambda e: e.tensor_tensor(
                                scrA[:], zi32[:], scrB[:], ALU.add))
                            qd = P.emit("vector", lambda e: e.tensor_tensor(
                                xqv[:, cs_], scrA[:], xn2t[:, cs_], ALU.subtract),
                                inc="c_q")
                            if pc == 1:
                                P.dma("sync", xq_hbm[tc * 128:(tc + 1) * 128, :],
                                      xqv[:], waits=[("c_q", qd)], inc="c_xqst")
                        return tq

                    return [t0, t1, mk_q(0), mk_q(1)]

                for qg in range(NQG):
                    qb = qg % 2
                    wq_ = [("c_qfm_free", qfm_free[qb])] if qg >= 2 else []
                    for kc in range(16):
                        P.dma_t("sync", qfm[qb][:, kc * GQ:(kc + 1) * GQ],
                                q_hbm[qg * GQ:(qg + 1) * GQ, kc * 128:(kc + 1) * 128],
                                waits=(wq_ if kc == 0 else []), inc="c_qld")
                    wk_ = [("c_kv_free", P.val("c_kv_free"))] if qg >= 1 else []
                    for kc in range(4):
                        P.dma_t("sync", kfm[:, kc * 1024:(kc + 1) * 1024],
                                k_hbm[qg * GQ:qg * GQ + 1024, kc * 128:(kc + 1) * 128],
                                waits=(wk_ if kc == 0 else []), inc="c_kld")
                    P.dma("sync", vtm.rearrange("p (c d) -> p c d", d=KV * HD),
                          v_hbm[qg * GQ:qg * GQ + 1024, :].rearrange("(c p) d -> p c d", p=128),
                          inc="c_kld")
                    c_qld_need = P.val("c_qld")
                    c_kld_need = P.val("c_kld")

                    steps = [(h, ktp) for h in range(H) for ktp in range(4)]

                    def emit_smm(h, ktp):
                        kv = h // 4
                        sb = ktp % 2
                        wS = [("c_qld", c_qld_need), ("c_kld", c_kld_need),
                              ("c_wld", c_wld_all), ("c_mld", c_mld_all)]
                        if P.val("c_tanh") >= 2:
                            wS.append(("c_tanh", P.val("c_tanh") - 1))
                        first = True
                        smm = None
                        for half in range(2):
                            kt = 2 * ktp + half
                            for dc in range(2):
                                kc4 = kv * 2 + dc
                                lhs = kfm[:, kc4 * 1024 + kt * 128: kc4 * 1024 + (kt + 1) * 128]
                                rhs = qfm[qb][:, (2 * h + dc) * GQ:(2 * h + dc + 1) * GQ]
                                smm = P.emit("tensor", lambda e, l=lhs, r=rhs, sb=sb, dc=dc, half=half:
                                             e.matmul(ps_st[sb][:, half * GQ:(half + 1) * GQ],
                                                      l, r, start=(dc == 0), stop=(dc == 1)),
                                             waits=(wS if first else []),
                                             inc=("c_smm" if (half == 1 and dc == 1) else None))
                                first = False
                        return smm

                    smm_next = emit_smm(*steps[0])
                    for idx, (h, ktp) in enumerate(steps):
                        kv = h // 4
                        sb = ktp % 2
                        smm = smm_next
                        if True:
                            wT = [("c_smm", smm)]
                            if P.val("c_mask") >= 2:
                                wT.append(("c_mask", P.val("c_mask") - 1))
                            tnh = P.emit("scalar", lambda e, sb=sb: e.activation(
                                th_s[:], ps_st[sb][:], AF.Tanh, scale=cc_(IC_1480)),
                                waits=wT, inc="c_tanh")
                            wM = [("c_tanh", tnh)]
                            if P.val("c_exp") >= 2:
                                wM.append(("c_exp", P.val("c_exp") - 1))
                            msk = P.emit("vector", lambda e, ktp=ktp: e.tensor_tensor(
                                u_s[:], th_s[:],
                                mask_sb[:, 2 * ktp * GQ:(2 * ktp + 2) * GQ],
                                ALU.add), waits=wM, inc="c_mask")
                            wE = [("c_mask", msk)]
                            if P.val("c_ptfree") >= 3:
                                wE.append(("c_ptfree", P.val("c_ptfree") - 3))
                            ex = P.emit("scalar", lambda e, ktp=ktp: e.activation(
                                pt_sb[:, 2 * ktp * GQ:(2 * ktp + 2) * GQ], u_s[:], AF.Exp,
                                scale=cc_(IC_30), bias=cc_(IC_M30)),
                                waits=wE, inc="c_exp")
                            if idx + 1 < len(steps):
                                smm_next = emit_smm(*steps[idx + 1])
                            for half in range(2):
                                kt = 2 * ktp + half
                                P.emit("tensor", lambda e, kt=kt: e.matmul(
                                    ps_z[:], ones_sb[:], pt_sb[:, kt * GQ:(kt + 1) * GQ],
                                    start=(kt == 0), stop=(kt == 7)),
                                    waits=[("c_exp", ex), ("c_zfree", P.val("c_zfree"))] if kt == 0 else ([("c_exp", ex)] if half == 0 else []))
                                for pvh in range(2):
                                    lhsv = vtm[:, kt * 512 + kv * 256 + pvh * 128:
                                               kt * 512 + kv * 256 + (pvh + 1) * 128]
                                    P.emit("tensor", lambda e, l=lhsv, pvh=pvh, kt=kt: e.matmul(
                                        ps_pv[pvh][:], l, pt_sb[:, kt * GQ:(kt + 1) * GQ],
                                        start=(kt == 0), stop=(kt == 7)),
                                        waits=([("c_pvfree", P.val("c_pvfree"))] if (kt == 0 and pvh == 0) else []),
                                        inc=("c_ptfree" if (half == 1 and pvh == 1) else None))
                        if ktp == 3:
                            zpv = P.emit("tensor", lambda e: e.nop(),
                                         waits=[("c_ptfree", P.val("c_ptfree"))], inc="c_zpv")
                            P.emit("vector", lambda e: e.reciprocal(z_a[:], ps_z[:]),
                                   waits=[("c_zpv", zpv)], inc="c_zfree")
                            wafm = []
                            if qg > 0:
                                wafm.append(("c_oproj", P.val("c_oproj")))
                            for half in range(2):
                                P.emit("vector", lambda e, h=h, half=half: e.scalar_tensor_tensor(
                                    afm[:, (2 * h + half) * GQ:(2 * h + half + 1) * GQ],
                                    ps_pv[half][:], 1.0, z_a[:], ALU.mult, ALU.mult),
                                    waits=(wafm if half == 0 else []),
                                    inc=("c_pvfree" if half == 1 else None))
                            P.emit("vector", lambda e: e.nop(), nodrain=True,
                                   waits=[("c_pvfree", P.val("c_pvfree"))], inc="c_afm")
                        if pend_tasks:
                            pend_tasks.pop(0)()
                    c_afm_all = P.val("c_afm")

                    last_tasks = []
                    for tl in range(4):
                        tc = qg * 4 + tl
                        rb = 0
                        wr = [("c_xfree", P.val("c_xfree"))] if tc >= 1 else []
                        xl = P.dma("sync", xres[rb][:],
                                   x_tm[HALO + tc * 128:HALO + (tc + 1) * 128, :],
                                   waits=wr, inc="c_xld")
                        for h2 in range(4):
                            om = None
                            for kc in range(16):
                                lhs = afm[:, kc * GQ + tl * 128: kc * GQ + (tl + 1) * 128]
                                rhs = wo_sb[:, kc * DM + h2 * 512: kc * DM + (h2 + 1) * 512]
                                om = P.emit("tensor", lambda e, l=lhs, r=rhs, kc=kc:
                                            e.matmul(ps_o[:], l, r,
                                                     start=(kc == 0), stop=(kc == 15)),
                                            waits=([("c_afm", c_afm_all),
                                                    ("c_oev", P.val("c_oev"))]
                                                   if kc == 0 else []),
                                            inc=("c_omm" if kc == 15 else None))
                            wadd = [("c_omm", om), ("c_xld", xl),
                                    ("c_x1st", P.val("c_x1st"))]
                            P.emit("vector", lambda e, h2=h2, rb=rb: e.tensor_tensor(
                                x1t[:, h2 * 512:(h2 + 1) * 512], ps_o[:],
                                xres[rb][:, h2 * 512:(h2 + 1) * 512], ALU.add),
                                waits=wadd, inc="c_oev")
                        xd = P.emit("vector", lambda e: e.nop(), nodrain=True,
                                    waits=[("c_oev", P.val("c_oev"))], inc="c_xfree")
                        P.dma("sync", x1_hbm[tc * 128:(tc + 1) * 128, :], x1t[:],
                              waits=[("c_oev", P.val("c_oev"))], inc="c_x1st")
                        if qg == NQG - 1:
                            for t in last_tasks:
                                t()
                            last_tasks = ln2_tasks(tc, P.val("c_x1st"))
                        else:
                            pend_tasks.extend(ln2_tasks(tc, P.val("c_x1st")))
                    for t in last_tasks:
                        t()
                    qfm_free[qb] = P.emit(
                        "tensor", lambda e: e.nop(),
                        waits=[("c_smm", P.val("c_smm"))], inc="c_qfm_free")
                    P.emit("tensor", lambda e: e.nop(),
                           waits=[("c_omm", P.val("c_omm"))], inc="c_oproj")
                    P.emit("tensor", lambda e: e.nop(),
                           waits=[("c_ptfree", P.val("c_ptfree"))], inc="c_kv_free")
                    if qg == 0:
                        P.dma("sync", mask_sb.rearrange("p (t q) -> p t q", q=GQ),
                              masks[0:8, :, :].rearrange("t p q -> p t q"),
                              waits=[("c_exp", P.val("c_exp"))], inc="c_mld")
                        c_mld_all = P.val("c_mld")
                while pend_tasks:
                    pend_tasks.pop(0)()
                sdd = P.emit("scalar", lambda e: e.activation(
                    varc[:], varc[:], AF.Sqrt, bias=cc_(IC_EPS5)),
                    waits=[("pv", P.val("pv"))], inc="psd")
                P.emit("vector", lambda e: e.reciprocal(tmpc2[:], varc[:]),
                       waits=[("psd", sdd)])
                P.emit("vector", lambda e: e.tensor_tensor(
                    amax2c[:, 0:NT], amax2c[:, 0:NT], tmpc2[:], ALU.mult))
                P.emit("vector", lambda e: e.tensor_scalar(
                    amax2c[:, 0:NT], amax2c[:, 0:NT], 1e-5, None, ALU.max))
            P.barrier("p1c")

        def _ph_2():
            NGRP = 8
            GH = HID // NGRP
            with contextlib.ExitStack() as ph:
                phase_reset()
                xqfm = A16.take(16 * DM)
                w1p = [A32.take(16 * 128) for i in range(2)]
                w1q = [A16.take(16 * GH) for i in range(2)]
                sa_s = A16.take(16 * 128)
                sb_s = A16.take(16 * 128)
                hy = A32.take(GH)
                ht = [A32.take(GH) for i in range(2)]
                w2p = [A32.take(DM) for i in range(2)]
                w2a = A16.take(DM)
                w2b = A16.take(DM)
                w2v = [A16.take(DM) for i in range(1)]
                tmp2 = A32.take(2)
                amaxg = A32.take(128)
                ps_f = [ph.enter_context(nc.psum_tensor(f"ps_f{i}", [128, GH], F32)) for i in range(2)]
                ps_bc = ph.enter_context(nc.psum_tensor("ps_bc2", [128, 1], F32))

                for kc in range(16):
                    P.dma_t("sync", xqfm[:, kc * DM:(kc + 1) * DM],
                            xq_hbm[:, kc * 128:(kc + 1) * 128], inc="f_xq")
                f_xq_all = P.val("f_xq")
                for tc in range(NT):
                    P.emit("vector", lambda e, tc=tc: e.scalar_tensor_tensor(
                        fac2[:, tc:tc + 1], amax2c[:, tc:tc + 1], 0.5 / 127.0,
                        w1m_col[:], ALU.mult, ALU.mult),
                        waits=([("w1m_done", P.val("w1m_done"))] if tc == 0 else []),
                        inc="f_fac")
                f_fac_all = P.val("f_fac")

                def w2_mean_slice(rc):
                    pb = rc % 2
                    w = [("w2ab_done", max(0, P.val("w2ab_done") - 1))] if rc >= 2 else []
                    ld = P.dma("gpsimd", w2p[pb][:], w2T[rc * 128:(rc + 1) * 128, :],
                               waits=w, inc="w2_ld%d" % pb)
                    P.emit("scalar", lambda e, pb=pb, rc=rc: e.activation(
                        w2p[pb][:], w2p[pb][:], AF.Abs,
                        accum_out=wsum2[:, rc:rc + 1]), waits=[("w2_ld%d" % pb, ld)],
                        inc="w2ab_done")

                def w2_mean_finish():
                    P.emit("vector", lambda e: e.reduce_sum(
                        tmp2[:, 0:1].unsqueeze(2),
                        wsum2[:, :].unsqueeze(1), AX.X),
                        waits=[("w2ab_done", P.val("w2ab_done"))], inc="w2s")
                    m1 = P.emit("tensor", lambda e: e.matmul(
                        ps_bc[0:1, 0:1], onesf_sb[:, 0:1], tmp2[:, 0:1],
                        start=True, stop=True), waits=[("w2s", P.val("w2s"))], inc="w2mm")
                    e1 = P.emit("scalar", lambda e: e.copy(tmp2[0:1, 1:2], ps_bc[0:1, 0:1]),
                                waits=[("w2mm", m1)], inc="w2ev")
                    m2 = P.emit("tensor", lambda e: e.matmul(
                        ps_bc[:, 0:1], onesf_sb[0:1, :], tmp2[0:1, 1:2],
                        start=True, stop=True), waits=[("w2ev", e1)], inc="w2mm")
                    e2 = P.emit("scalar", lambda e: e.copy(w2m_col[:], ps_bc[:, 0:1]),
                                waits=[("w2mm", m2)], inc="w2ev")
                    P.emit("vector", lambda e: e.tensor_scalar(
                        w2m_col[:], w2m_col[:], 1.0 / (DM * HID), 1e-5, ALU.mult, ALU.max),
                        waits=[("w2ev", e2)])
                    P.emit("vector", lambda e: e.reciprocal(w2inv[:], w2m_col[:]),
                           inc="w2m_done")

                def w2_quant_slice(rc):
                    pb = rc % 2
                    w = [("w2q_st", P.val("w2q_st"))] if rc >= 2 else \
                        [("w2ab_done", P.val("w2ab_done"))]
                    ld = P.dma("gpsimd", w2p[pb][:], w2T[rc * 128:(rc + 1) * 128, :],
                               waits=w, inc="w2_ld2%d" % pb)
                    P.emit("vector", lambda e, pb=pb: e.tensor_scalar(
                        w2a[:], w2p[pb][:], w2inv[:], 0.5, ALU.mult, ALU.is_ge),
                        waits=[("w2_ld2%d" % pb, ld), ("w2m_done", P.val("w2m_done"))])
                    P.emit("vector", lambda e, pb=pb: e.tensor_scalar(
                        w2b[:], w2p[pb][:], w2inv[:], -0.5, ALU.mult, ALU.is_ge))
                    vq = P.emit("vector", lambda e: e.scalar_tensor_tensor(
                        w2v[0][:], w2a[:], -1.0, w2b[:], ALU.add, ALU.add),
                        waits=[("w2q_st", P.val("w2q_st"))], inc="w2_vq")
                    P.dma("sync", w2q_hbm[rc * 128:(rc + 1) * 128, :], w2v[0][:],
                          waits=[("w2_vq", vq)], inc="w2q_st")

                def w1_piece(g, pi):
                    gb = g % 2
                    idx = g * 8 + pi
                    pb = idx % 2
                    w = [("f_w1q", P.val("f_w1q"))] if idx >= 2 else []
                    ld = P.dma("gpsimd",
                               w1p[pb].rearrange("p (kc j) -> p kc j", j=128),
                               w1T[:, g * GH + pi * 128: g * GH + (pi + 1) * 128]
                               .rearrange("(kc p) j -> p kc j", p=128),
                               waits=w, inc="f_w1ld%d" % pb)
                    wtq = [("f_w1ld%d" % pb, ld), ("w1m_done", P.val("w1m_done"))]
                    if idx >= 2:
                        wtq.append(("f_sg", 2 * idx - 2))
                    tqd = P.emit("vector", lambda e, pb=pb: e.tensor_scalar(
                        w1p[pb][:], w1p[pb][:], w1inv[:], None, ALU.mult),
                        waits=wtq, inc="f_tq")
                    sad = P.emit("scalar", lambda e, pb=pb: e.activation(
                        sa_s[:], w1p[pb][:], AF.Sign, bias=cc_(IC_M05)),
                        waits=[("f_tq", tqd), ("f_w1q", P.val("f_w1q"))],
                        inc="f_sg")
                    sbd = P.emit("scalar", lambda e, pb=pb: e.activation(
                        sb_s[:], w1p[pb][:], AF.Sign, bias=cc_(IC_P05)), inc="f_sg")
                    wmm = [("f_mm_g", P.val("f_mm_g"))] if g >= 2 else []
                    P.emit("vector", lambda e, gb=gb, pi=pi: e.tensor_tensor(
                        w1q[gb].rearrange("p (kc j) -> p kc j", j=GH)[:, :, pi * 128:(pi + 1) * 128],
                        sa_s.rearrange("p (kc j) -> p kc j", j=128),
                        sb_s.rearrange("p (kc j) -> p kc j", j=128), ALU.add),
                        waits=wmm + [("f_sg", sbd)], inc="f_w1q")

                amax_init = [False] * NT
                for pi in range(8):
                    w1_piece(0, pi)
                for g in range(NGRP):
                    gb = g % 2
                    f_w1q_need = P.val("f_w1q")

                    for tc in range(NT):
                        if tc % 2 == 0 and g + 1 < NGRP:
                            w1_piece(g + 1, tc // 2)
                        slot = g * NT + tc
                        if slot < 64:
                            w2_mean_slice(slot)
                        elif slot == 64:
                            w2_mean_finish()
                        elif slot - 65 < 64:
                            w2_quant_slice(slot - 65)
                        tb = tc % 2
                        wl = [("f_w1q", f_w1q_need), ("f_xq", f_xq_all)]
                        if P.val("f_ev") >= 2:
                            wl.append(("f_ev", P.val("f_ev") - 1))
                        first = True
                        for kc in range(16):
                            lhs = xqfm[:, kc * DM + tc * 128: kc * DM + (tc + 1) * 128]
                            for nb in range(2):
                                rhs = w1q[gb][:, kc * GH + nb * 512: kc * GH + (nb + 1) * 512]
                                fmm = P.emit("tensor", lambda e, l=lhs, r=rhs, nb=nb, kc=kc, tb=tb:
                                             e.matmul(ps_f[tb][:, nb * 512:(nb + 1) * 512],
                                                      l, r, start=(kc == 0), stop=(kc == 15)),
                                             waits=(wl if first else []),
                                             inc=("f_mm" if (kc == 15 and nb == 1) else None))
                                first = False
                        wep = [("f_mm", fmm), ("f_fac", f_fac_all)]
                        whst = [("f_hst", P.val("f_hst"))] if P.val("f_hst") > 0 else []
                        P.emit("scalar", lambda e, tb=tb, tc=tc: e.activation(
                            hy[:], ps_f[tb][:], AF.Relu, scale=fac2[:, tc:tc + 1]),
                            waits=wep, inc="f_ev")
                        hb = tc % 2
                        hd_ = P.emit("scalar", lambda e, hb=hb: e.activation(
                            ht[hb][:], hy[:], AF.Square),
                            waits=whst, inc="f_h")
                        P.emit("vector", lambda e, tc=tc, hb=hb, g=g: e.tensor_reduce(
                            amaxg[:, g * NT + tc: g * NT + tc + 1].unsqueeze(2),
                            ht[hb][:, :].unsqueeze(1), AX.X, ALU.max),
                            waits=[("f_h", hd_)])
                        P.dma("sync", h_hbm[tc * 128:(tc + 1) * 128, g * GH:(g + 1) * GH],
                              ht[hb][:], waits=[("f_h", hd_)], inc="f_hst")
                        if g == NGRP - 1:
                            P.emit("vector", lambda e, tc=tc: e.tensor_reduce(
                                amax3r[:, tc:tc + 1].unsqueeze(2),
                                amaxg.rearrange("p (g t) -> p t g", t=NT)[:, tc:tc + 1, :],
                                AX.X, ALU.max))
                            P.emit("vector", lambda e, tc=tc: e.tensor_scalar(
                                amax3c[:, tc:tc + 1], amax3r[:, tc:tc + 1], 1e-5, None, ALU.max))
                            P.emit("vector", lambda e, tc=tc: e.reciprocal(
                                tmp2[:, 1:2], amax3c[:, tc:tc + 1]))
                            P.emit("vector", lambda e, tc=tc: e.tensor_scalar(
                                s3col[:, tc:tc + 1], tmp2[:, 1:2], 127.0, None, ALU.mult))
                            P.emit("vector", lambda e, tc=tc: e.scalar_tensor_tensor(
                                fac3[:, tc:tc + 1], amax3c[:, tc:tc + 1], 1.0 / 127.0,
                                w2m_col[:], ALU.mult, ALU.mult),
                                waits=[("w2m_done", P.val("w2m_done"))], inc="f_s3")
                    P.emit("tensor", lambda e: e.nop(),
                           waits=[("f_mm", P.val("f_mm"))], inc="f_mm_g")
                w2_quant_slice(63)
            P.barrier("p2")

        def _ph_3():
            NSC = 8
            SCW = HID // NSC
            with contextlib.ExitStack() as ph:
                phase_reset()
                w2sb = [A16.take(8 * DM) for i in range(2)]
                hqfm = [A16.take(8 * DM) for i in range(2)]
                hpc = [A32.take(SCW) for i in range(2)]
                qA = A32.take(SCW)
                qB = A32.take(SCW)
                qI = AI.take(SCW)
                hqv = [A16.take(SCW) for i in range(2)]
                ev = [A32.take(DM) for i in range(1)]
                x1c = A32.take(DM)
                yac = A32.take(DM)
                yst = A32.take(DM)
                ps3 = [ph.enter_context(nc.psum_tensor(f"ps3{i}", [128, DM], F32)) for i in range(2)]

                def quant_hq(sc, tc):
                    i = sc * NT + tc
                    pb = i % 2
                    w = [("g_q", P.val("g_q"))] if i >= 2 else []
                    ld = P.dma("sync", hpc[pb][:],
                               h_hbm[tc * 128:(tc + 1) * 128, sc * SCW:(sc + 1) * SCW],
                               waits=w, inc="g_hld")
                    P.emit("vector", lambda e, pb=pb, tc=tc: e.tensor_scalar(
                        qA[:], hpc[pb][:], s3col[:, tc:tc + 1], None, ALU.mult),
                        waits=[("g_hld", ld), ("f_s3", P.val("f_s3"))])
                    P.emit("vector", lambda e: e.tensor_copy(qI[:], qA[:]))
                    P.emit("vector", lambda e: e.scalar_tensor_tensor(
                        qB[:], qA[:], 0.5, qI[:], ALU.subtract, ALU.is_ge))
                    wst = [("g_hqst", P.val("g_hqst"))] if i >= 2 else []
                    qd = P.emit("vector", lambda e, pb=pb: e.tensor_tensor(
                        hqv[pb][:], qI[:], qB[:], ALU.add),
                        waits=wst, inc="g_q")
                    P.dma("sync", hq_hbm[tc * 128:(tc + 1) * 128, sc * SCW:(sc + 1) * SCW],
                          hqv[pb][:], waits=[("g_q", qd)], inc="g_hqst")

                for tc in range(NT):
                    quant_hq(0, tc)

                for sc in range(NSC):
                    g_hq_need = P.val("g_hqst")

                    sb = sc % 2
                    wfree = [("g_w2free", P.val("g_w2free"))] if sc >= 2 else [("w2q_st", P.val("w2q_st"))]
                    for kc in range(8):
                        P.dma("gpsimd", w2sb[sb][:, kc * DM:(kc + 1) * DM],
                              w2q_hbm[sc * SCW + kc * 128: sc * SCW + (kc + 1) * 128, :],
                              waits=(wfree if kc == 0 else []), inc="g_w2ld")
                    whq = [("g_hqfree", P.val("g_hqfree"))] if sc >= 2 else []
                    for kc in range(8):
                        P.dma_t("scalar", hqfm[sb][:, kc * DM:(kc + 1) * DM],
                                hq_hbm[:, sc * SCW + kc * 128: sc * SCW + (kc + 1) * 128],
                                waits=(whq if kc == 0 else []) +
                                      ([("g_hqst", g_hq_need)] if kc == 0 else []),
                                inc="g_hqld")
                    g_ld_need = P.val("g_w2ld")
                    g_hqld_need = P.val("g_hqld")

                    for tc in range(NT):
                        tb = tc % 2
                        wl = [("g_w2ld", g_ld_need), ("g_hqld", g_hqld_need)]
                        if P.val("g_ev") >= 2:
                            wl.append(("g_ev", P.val("g_ev") - 1))
                        first = True
                        for kc in range(8):
                            lhs = hqfm[sb][:, kc * DM + tc * 128: kc * DM + (tc + 1) * 128]
                            for nb in range(4):
                                rhs = w2sb[sb][:, kc * DM + nb * 512: kc * DM + (nb + 1) * 512]
                                gmm = P.emit("tensor", lambda e, l=lhs, r=rhs, nb=nb, kc=kc, tb=tb:
                                             e.matmul(ps3[tb][:, nb * 512:(nb + 1) * 512],
                                                      l, r, start=(kc == 0), stop=(kc == 7)),
                                             waits=(wl if first else []),
                                             inc=("g_mm" if (kc == 7 and nb == 3) else None))
                                first = False
                        if sc + 1 < NSC:
                            quant_hq(sc + 1, tc)
                        eb = 0
                        acsem = "g_ac%02d" % tc
                        wev = [("g_mm", P.val("g_mm"))]
                        if P.val("g_evf%d" % eb) > 0:
                            wev.append(("g_evf%d" % eb, P.val("g_evf%d" % eb)))
                        evd = P.emit("vector", lambda e, tb=tb, eb=eb: e.tensor_copy(
                            ev[eb][:], ps3[tb][:]),
                            waits=wev, inc="g_ev")
                        if sc == 0:
                            P.emit("gpsimd", lambda e, tc=tc, eb=eb: e.dma_start(
                                out=yacc_hbm[tc * 128:(tc + 1) * 128, :], in_=ev[eb][:]),
                                waits=[("g_ev", evd)], inc=acsem, inc_n=16)
                        else:
                            P.emit("gpsimd", lambda e, tc=tc, eb=eb: e.dma_start(
                                out=yacc_hbm[tc * 128:(tc + 1) * 128, :], in_=ev[eb][:],
                                accum_op=ALU.add),
                                waits=[("g_ev", evd), (acsem, P.val(acsem))],
                                inc=acsem, inc_n=16)
                        P.emit("gpsimd", lambda e: e.nop(),
                               waits=[(acsem, P.val(acsem))], inc="g_evf%d" % eb)
                        if sc == NSC - 1:
                            wld = [("g_yst", P.val("g_yst"))] if tc >= 2 else []
                            P.dma("sync", yac[:], yacc_hbm[tc * 128:(tc + 1) * 128, :],
                                  waits=[("g_ac%02d" % tc, P.val("g_ac%02d" % tc))] + wld,
                                  inc="g_yld")
                            P.dma("sync", x1c[:], x1_hbm[tc * 128:(tc + 1) * 128, :],
                                  inc="g_yld")
                            fy = P.emit("vector", lambda e, tc=tc: e.scalar_tensor_tensor(
                                yst[:], yac[:], fac3[:, tc:tc + 1], x1c[:],
                                ALU.mult, ALU.add),
                                waits=[("g_yld", P.val("g_yld")),
                                       ("g_yst", P.val("g_yst"))], inc="g_y")
                            P.dma("sync", y_out[tc * 128:(tc + 1) * 128, :], yst[:],
                                  waits=[("g_y", fy)], inc="g_yst")
                    P.emit("tensor", lambda e: e.nop(),
                           waits=[("g_mm", P.val("g_mm"))], inc="g_w2free")
                    P.emit("tensor", lambda e: e.nop(),
                           waits=[("g_mm", P.val("g_mm"))], inc="g_hqfree")
            P.emit("sync", None, waits=[("g_yst", P.val("g_yst"))])

        if "1b" in phases:
            _ph_1b()
        if "1c" in phases:
            _ph_1c()
        if "2" in phases:
            _ph_2()
        if "3" in phases:
            _ph_3()

        with nc.Block() as block:
            P.replay(block)

    return nc



def _rope_tables():
    inv_freq = (1.0 / (THETA ** (np.arange(0, HD, 2, dtype=np.float64) / HD))
                ).astype(np.float32)
    pos = np.arange(S, dtype=np.float32)
    freqs = (pos[:, None] * inv_freq[None, :]).astype(np.float32)
    a = freqs.astype(np.float64)
    return np.cos(a).astype(np.float32), np.sin(a).astype(np.float32)


def _masks(hf):
    m = np.zeros((16, 128, GQ), dtype=np.float32)
    kp = np.arange(128)[:, None]
    qf = np.arange(GQ)[None, :]
    for kt in range(8):
        d = kt * 128 + kp - qf
        allowed = (d >= 1) & (d <= SW)
        m[kt] = np.where(allowed, 0.0, -3334.0)
        if hf == 0:
            al0 = allowed & (kt * 128 + kp >= HALO)
            m[8 + kt] = np.where(al0, 0.0, -3334.0)
        else:
            m[8 + kt] = m[kt]
    return m.astype(ml_dtypes.bfloat16)


def make_in_maps(L0_attn_hidden_states, L0_attn_Wq, L0_attn_Wk, L0_attn_Wv,
                 L0_attn_Wo, L0_ffn_w1, L0_ffn_w2):
    x = np.asarray(L0_attn_hidden_states, dtype=np.float32)
    wq = np.ascontiguousarray(np.asarray(L0_attn_Wq, dtype=np.float32).T)
    wk = np.ascontiguousarray(np.asarray(L0_attn_Wk, dtype=np.float32).T)
    wv = np.ascontiguousarray(np.asarray(L0_attn_Wv, dtype=np.float32).T)
    wo = np.ascontiguousarray(np.asarray(L0_attn_Wo, dtype=np.float32).T)
    w1 = np.ascontiguousarray(np.asarray(L0_ffn_w1, dtype=np.float32).T)
    w2 = np.ascontiguousarray(np.asarray(L0_ffn_w2, dtype=np.float32).T)
    cosT, sinT = _rope_tables()
    consts = np.tile(np.asarray(CONSTS, dtype=np.float32)[None, :], (128, 1))
    ones16 = np.ones((128, 128), dtype=ml_dtypes.bfloat16)
    in_maps = []
    for core in range(N_CORES):
        b, hf = core // 2, core % 2
        xc = np.zeros((C, DM), dtype=np.float32)
        lo = hf * T - HALO
        src_lo = max(lo, 0)
        xc[src_lo - lo:, :] = x[b, src_lo:hf * T + T, :]
        p0 = hf * T - HALO
        idx = np.clip(np.arange(p0, p0 + C), 0, S - 1)
        in_maps.append({
            "x_tm": np.ascontiguousarray(xc),
            "wqT": wq, "wkT": wk, "wvT": wv, "woT": wo,
            "w1T": w1, "w2T": w2,
            "ropec": np.ascontiguousarray(cosT[idx]),
            "ropes": np.ascontiguousarray(sinT[idx]),
            "masks": _masks(hf), "onesb": ones16, "consts": consts,
        })
    return in_maps


def kernel(L0_attn_hidden_states, L0_attn_Wq, L0_attn_Wk, L0_attn_Wv,
           L0_attn_Wo, L0_ffn_w1, L0_ffn_w2):
    in_maps = make_in_maps(L0_attn_hidden_states, L0_attn_Wq, L0_attn_Wk,
                           L0_attn_Wv, L0_attn_Wo, L0_ffn_w1, L0_ffn_w2)
    nc = build_program()
    res = run_bass_kernel_spmd(nc, in_maps, core_ids=list(range(N_CORES)))
    out = np.empty((B, S, DM), dtype=np.float32)
    for core in range(N_CORES):
        b, hf = core // 2, core % 2
        out[b, hf * T:(hf + 1) * T, :] = res.results[core]["y"]
    return out


# version 18
# speedup vs baseline: 1.0301x; 1.0301x over previous
import numpy as np
import ml_dtypes
import concourse.bass as bass
import concourse.mybir as mybir
from concourse.bass_utils import run_bass_kernel_spmd

F32 = mybir.dt.float32
BF16 = mybir.dt.bfloat16
I32 = mybir.dt.int32
AF = mybir.ActivationFunctionType
ALU = mybir.AluOpType
AX = mybir.AxisListType

B, S, DM = 4, 4096, 2048
H, KV, HD = 8, 2, 256
SW = 512
T = 2048
HALO = 512
C = T + HALO
NT = T // 128
NCC = C // 128
HID = 8192
THETA = 1e6
NQG = 4
GQ = 512
N_CORES = 8

CONSTS = [-0.5, 0.5, 1e-6, 1e-5, -30.0, 30.0, 1.0 / 480.0]
IC_M05, IC_P05, IC_EPS6, IC_EPS5, IC_M30, IC_30, IC_1480 = range(7)

ENGS = ("sync", "vector", "scalar", "tensor", "gpsimd")


class _RecIns:
    def then_inc(self, *a, **k):
        return self


class _Rec:
    _SKIP = {"nop", "dma_start", "dma_start_transpose", "drain", "wait_ge"}

    def __init__(self):
        self.r, self.w = [], []
        self.conservative = False

    def __getattr__(self, name):
        def h(*a, **k):
            if name in _Rec._SKIP:
                return _RecIns()
            if "accum_out" in k and k["accum_out"] is not None:
                self.conservative = True
            aps = [x for x in a if hasattr(x, "tensor")]
            if aps:
                self.w.append(aps[0])
                self.r += aps[1:]
            for kk, v in k.items():
                if hasattr(v, "tensor"):
                    (self.w if kk in ("accum_out", "out") else self.r).append(v)
            return _RecIns()
        return h


class Prog:

    CROSS = {"init_ld", "init_v", "w1m_done", "f_s3", "w2q_st", "bar",
             "b_wld", "w1_ld0", "w1_ld1", "c_wld", "w2_ld0", "w2_ld1",
             "w2_ld20", "w2_ld21", "f_w1ld0", "f_w1ld1", "g_w2ld", "g_acst"} | {"g_ac%02d" % i for i in range(16)}

    def __init__(self, nc):
        self.nc = nc
        self.ops = {e: [] for e in ENGS}
        self.sems = {}
        self.cnt = {}
        self._local = {}
        self.guar = {}
        self.acc_w = {}
        self.acc_r = {}

    @staticmethod
    def _ivs(aps):
        out = []
        for ap in aps:
            try:
                reach = 1
                for s, n in ap.ap[1:]:
                    reach += abs(s) * (n - 1)
                off = ap.offset
                out.append((ap.tensor.name, off, off + reach))
            except Exception:
                return None
        return out

    @staticmethod
    def _clash(ws, rw):
        if ws is None or rw is None:
            return True
        for an, al, ah in ws:
            for bn, bl, bh in rw:
                if an == bn and al < bh and bl < ah:
                    return True
        return False

    def new_phase(self):
        self._local = {}

    def alias(self, name):
        if name.startswith("pool") or name == "bar":
            return name
        if name.startswith("bar_"):
            return "bar"
        if name in self.CROSS:
            return name
        if name not in self._local:
            self._local[name] = "pool%02d" % len(self._local)
        return self._local[name]

    def sem(self, name):
        name = self.alias(name)
        if name not in self.sems:
            self.sems[name] = self.nc.semaphore(name).__enter__()
            self.cnt[name] = 0
        return self.sems[name]

    def emit(self, eng, fn, waits=(), inc=None, inc_n=1, nodrain=False,
             sem_ok=False):
        g = self.guar.setdefault(eng, {})
        ws = []
        for s, v in waits:
            s = self.alias(s)
            if g.get(s, -1) >= v:
                continue
            g[s] = v
            ws.append((s, v))
            self.sem(s)
        if eng in ("vector", "scalar") and fn is not None:
            if nodrain:
                cur_r, cur_w = [], []
            else:
                try:
                    rec = _Rec()
                    fn(rec)
                    if rec.conservative:
                        cur_w = cur_r = None
                    else:
                        cur_w = self._ivs(rec.w)
                        cur_r = self._ivs(rec.r)
                except Exception:
                    cur_w = cur_r = None
            acc_w = self.acc_w.get(eng, [])
            acc_r = self.acc_r.get(eng, [])
            if cur_w is None or cur_r is None:
                cur_rw = None
            else:
                cur_rw = cur_w + cur_r
            drain_before = (not sem_ok and
                            (self._clash(acc_w, cur_rw) or
                             self._clash(acc_r, cur_w)))
            if drain_before:
                self.acc_w[eng] = cur_w
                self.acc_r[eng] = cur_r
            else:
                self.acc_w[eng] = (None if (acc_w is None or cur_w is None)
                                   else acc_w + cur_w)
                self.acc_r[eng] = (None if (acc_r is None or cur_r is None)
                                   else acc_r + cur_r)
        else:
            drain_before = False
        entry = [fn, ws, None, drain_before]
        if inc is not None:
            inc = self.alias(inc)
            self.sem(inc)
            self.cnt[inc] += inc_n
            entry[2] = (inc, inc_n)
        self.ops[eng].append(entry)
        return self.cnt[inc] if inc is not None else None

    def val(self, sem_name):
        self.sem(sem_name)
        return self.cnt[self.alias(sem_name)]

    def dma(self, eng, out, in_, waits=(), inc=None):
        return self.emit(eng, lambda e, o=out, i=in_: e.dma_start(out=o, in_=i),
                         waits, inc, 16, nodrain=True)

    def dma_t(self, eng, out, in_, waits=(), inc=None):
        return self.emit(eng, lambda e, o=out, i=in_: e.dma_start_transpose(o, i),
                         waits, inc, 16, nodrain=True)

    def barrier(self, tag):
        s = "bar"
        self.sem(s)
        flush = [(n, v) for n, v in self.cnt.items()
                 if v > 0 and not n.startswith("bar_")]
        for n, v in flush:
            self.emit("sync", None, waits=[(n, v)])
        tgt = self.cnt[s] + len(ENGS)
        for e in ENGS:
            self.emit(e, lambda eng: eng.nop(), inc=s)
        for e in ENGS:
            self.emit(e, None, waits=[(s, tgt)])
        return tgt

    def replay(self, block):
        def mk(name):
            def body(eng):
                for fn, waits, inc, drain_before in self.ops[name]:
                    for s, v in waits:
                        eng.wait_ge(self.sems[s], v)
                    if fn is None:
                        continue
                    if drain_before:
                        eng.drain()
                    ins = fn(eng)
                    if inc is not None:
                        ins.then_inc(self.sems[inc[0]], inc[1])
            return body
        block.sync(mk("sync"))
        block.vector(mk("vector"))
        block.scalar(mk("scalar"))
        block.tensor(mk("tensor"))
        block.gpsimd(mk("gpsimd"))


class Arena:

    def __init__(self, nc, stack, name, dtype, cols):
        self.t = stack.enter_context(nc.sbuf_tensor(name, [128, cols], dtype))
        self.cols = cols
        self.off = 0

    def reset(self):
        self.off = 0

    def take(self, cols):
        assert self.off + cols <= self.cols, \
            f"arena {self.t.name} overflow: {self.off}+{cols}>{self.cols}"
        a = self.t[:, self.off:self.off + cols]
        self.off += cols
        return a


def build_program(dev_taps=False, phases=("1a", "1b", "1c", "2p", "2", "3")):
    nc = bass.Bass("TRN2", target_bir_lowering=False)
    skind = "ExternalOutput" if dev_taps else "Internal"

    x_tm = nc.dram_tensor("x_tm", [C, DM], F32, kind="ExternalInput").ap()
    wqT = nc.dram_tensor("wqT", [DM, DM], F32, kind="ExternalInput").ap()
    wkT = nc.dram_tensor("wkT", [DM, KV * HD], F32, kind="ExternalInput").ap()
    wvT = nc.dram_tensor("wvT", [DM, KV * HD], F32, kind="ExternalInput").ap()
    woT = nc.dram_tensor("woT", [DM, DM], F32, kind="ExternalInput").ap()
    w1T = nc.dram_tensor("w1T", [DM, HID], F32, kind="ExternalInput").ap()
    w2T = nc.dram_tensor("w2T", [HID, DM], F32, kind="ExternalInput").ap()
    ropec = nc.dram_tensor("ropec", [C, 128], F32, kind="ExternalInput").ap()
    ropes = nc.dram_tensor("ropes", [C, 128], F32, kind="ExternalInput").ap()
    masks = nc.dram_tensor("masks", [16, 128, GQ], BF16, kind="ExternalInput").ap()
    onesb = nc.dram_tensor("onesb", [128, 128], BF16, kind="ExternalInput").ap()
    consts_in = nc.dram_tensor("consts", [128, len(CONSTS)], F32,
                               kind="ExternalInput").ap()

    y_out = nc.dram_tensor("y", [T, DM], F32, kind="ExternalOutput").ap()

    xn_hbm = nc.dram_tensor("xn_hbm", [C, DM], BF16, kind=skind).ap()
    q_hbm = nc.dram_tensor("q_hbm", [T, DM], BF16, kind=skind).ap()
    k_hbm = nc.dram_tensor("k_hbm", [C, KV * HD], BF16, kind=skind).ap()
    v_hbm = nc.dram_tensor("v_hbm", [C, KV * HD], BF16, kind=skind).ap()
    x1_hbm = nc.dram_tensor("x1_hbm", [T, DM], F32, kind=skind).ap()
    xq_hbm = nc.dram_tensor("xq_hbm", [T, DM], BF16, kind=skind).ap()
    h_hbm = nc.dram_tensor("h_hbm", [T, HID], F32, kind=skind).ap()
    hq_hbm = nc.dram_tensor("hq_hbm", [T, HID], BF16, kind=skind).ap()
    w2q_hbm = nc.dram_tensor("w2q_hbm", [HID, DM], BF16, kind=skind).ap()
    yacc_hbm = nc.dram_tensor("yacc_hbm", [T, DM], F32, kind=skind).ap()

    P = Prog(nc)
    import contextlib
    top = contextlib.ExitStack()
    with top:
        consts = top.enter_context(nc.sbuf_tensor("consts_sb", [128, len(CONSTS)], F32))
        ones_sb = top.enter_context(nc.sbuf_tensor("ones_sb", [128, 128], BF16))
        onesf_sb = top.enter_context(nc.sbuf_tensor("onesf_sb", [128, 128], F32))
        amax2c = top.enter_context(nc.sbuf_tensor("amax2c", [128, NT], F32))
        s2col = top.enter_context(nc.sbuf_tensor("s2col", [128, NT], F32))
        amax3r = top.enter_context(nc.sbuf_tensor("amax3r", [128, NT], F32))
        amax3c = top.enter_context(nc.sbuf_tensor("amax3c", [128, NT], F32))
        s3col = top.enter_context(nc.sbuf_tensor("s3col", [128, NT], F32))
        fac2 = top.enter_context(nc.sbuf_tensor("fac2", [128, NT], F32))
        fac3 = top.enter_context(nc.sbuf_tensor("fac3", [128, NT], F32))
        w1m_col = top.enter_context(nc.sbuf_tensor("w1m_col", [128, 1], F32))
        w2m_col = top.enter_context(nc.sbuf_tensor("w2m_col", [128, 1], F32))
        w1inv = top.enter_context(nc.sbuf_tensor("w1inv", [128, 1], F32))
        w2inv = top.enter_context(nc.sbuf_tensor("w2inv", [128, 1], F32))
        wsum1 = top.enter_context(nc.sbuf_tensor("wsum1", [128, 16], F32))
        wsum2 = top.enter_context(nc.sbuf_tensor("wsum2", [128, 64], F32))
        bc_sc = top.enter_context(nc.sbuf_tensor("bc_sc", [128, 2], F32))
        A16 = Arena(nc, top, "arena_bf16", BF16, 77824)
        A32 = Arena(nc, top, "arena_f32", F32, 12544)
        AI = Arena(nc, top, "arena_i32", I32, 1024)

        def phase_reset():
            A16.reset(); A32.reset(); AI.reset()
            P.new_phase()

        cc_ = lambda i: consts[:, i:i + 1]

        P.dma("sync", consts[:], consts_in[:, :], inc="init_ld")
        P.dma("sync", ones_sb[:], onesb[:, :], inc="init_ld")
        init_done = P.val("init_ld")
        P.emit("vector", lambda e: e.tensor_copy(onesf_sb[:], ones_sb[:]),
               waits=[("init_ld", init_done)])
        P.emit("scalar", None, waits=[("init_ld", init_done)])

        def ln_stats(x_ap, W, st, sq_small, eps_col, tag, waits_first, inv_w,
                     defer_var=None):
            P.emit("vector", lambda e: e.reduce_sum(
                st[:, 0:1].unsqueeze(2),
                x_ap.unsqueeze(1), AX.X), waits=waits_first)
            hwp = W // 2
            P.emit("scalar", lambda e: e.activation(
                sq_small[:, :hwp], x_ap[:, 0:hwp], AF.Square,
                accum_out=st[:, 8:9]),
                waits=list(waits_first) + [(tag + "sqf", P.val(tag + "sqf"))])
            sqd = P.emit("scalar", lambda e: e.activation(
                sq_small[:, :hwp], x_ap[:, hwp:W], AF.Square,
                accum_out=st[:, 9:10]), inc=tag + "sq")
            P.emit("vector", lambda e: e.tensor_tensor(
                st[:, 1:2], st[:, 8:9], st[:, 9:10], ALU.add),
                waits=[(tag + "sq", sqd)], inc=tag + "sqf")
            P.emit("vector", lambda e: e.tensor_scalar(
                st[:, 2:4], st[:, 0:2], inv_w, None, ALU.mult))
            P.emit("vector", lambda e: e.tensor_tensor(
                st[:, 4:5], st[:, 2:3], st[:, 2:3], ALU.mult),
                waits=([(tag + "sd", P.val(tag + "sd"))]
                       if defer_var is None else []))
            if defer_var is not None:
                P.emit("vector", lambda e: e.tensor_tensor(
                    defer_var, st[:, 3:4], st[:, 4:5], ALU.subtract),
                    inc=tag + "v")
                return
            vd = P.emit("vector", lambda e: e.tensor_tensor(
                st[:, 5:6], st[:, 3:4], st[:, 4:5], ALU.subtract), inc=tag + "v")
            sdd = P.emit("scalar", lambda e: e.activation(
                st[:, 6:7], st[:, 5:6], AF.Sqrt, bias=eps_col),
                waits=[(tag + "v", vd)], inc=tag + "sd")
            P.emit("vector", lambda e: e.reciprocal(st[:, 7:8], st[:, 6:7]),
                   waits=[(tag + "sd", sdd)])

        def _ph_1a():
            xbuf = [A32.take(DM) for i in range(2)]
            sqsm = A32.take(DM // 2)
            xnb = [A16.take(DM) for i in range(2)]
            stat = A32.take(16)

            def emit_chunk(cc):
                b = cc % 2
                w = [("a_free", P.val("a_free"))] if cc >= 2 else []
                ld = P.dma("sync", xbuf[b][:], x_tm[cc * 128:(cc + 1) * 128, :],
                           waits=w, inc="a_ld")
                ln_stats(xbuf[b][:], DM, stat, sqsm, cc_(IC_EPS5), "a",
                         [("a_ld", ld)], 1.0 / DM)
                wapp = []
                if cc >= 2:
                    sprev = "a_st%d" % ((cc - 2) // 4)
                    wapp = [(sprev, P.val(sprev))]
                apd = P.emit("vector", lambda e, b=b: e.tensor_scalar(
                    xnb[b][:], xbuf[b][:], stat[:, 2:3], stat[:, 7:8],
                    ALU.subtract, ALU.mult), waits=wapp, inc="a_ap")
                P.emit("vector", lambda e: e.nop(), nodrain=True,
                       waits=[("a_ap", apd)], inc="a_free")
                P.dma("sync", xn_hbm[cc * 128:(cc + 1) * 128, :], xnb[b][:],
                      waits=[("a_ap", apd)], inc="a_st%d" % (cc // 4))
            return emit_chunk

        def _ph_1b():
            NG, GCH = 5, 4
            GTOK = GCH * 128
            with contextlib.ExitStack() as ph:
                phase_reset()
                wq_sb = A16.take(16 * DM)
                wk_sb = A16.take(16 * KV * HD)
                wv_sb = A16.take(16 * KV * HD)
                xnf = [A16.take(16 * GTOK)]
                cosb = [A32.take(GTOK) for i in range(2)]
                sinb = [A32.take(GTOK) for i in range(2)]
                qf = [A16.take(DM)]
                kf = [A16.take(KV * HD)]
                t1q = A16.take(DM + 2 * KV * HD)
                t2q = A16.take(DM)
                sqsm = A32.take(HD)
                qkvp = A16.take(DM + 2 * KV * HD)
                qkvo = A16.take(DM + 2 * KV * HD)
                stq = A32.take(64)
                w1p = [A32.take(DM) for i in range(2)]
                ps_q = ph.enter_context(nc.psum_tensor("ps_q", [128, DM], F32))
                ps_bc = ps_q
                ps_k = [ph.enter_context(nc.psum_tensor(f"ps_k{i}", [128, KV * HD], F32)) for i in range(2)]
                ps_v = [ph.enter_context(nc.psum_tensor(f"ps_v{i}", [128, KV * HD], F32)) for i in range(2)]

                for kc in range(16):
                    P.dma("gpsimd", wq_sb[:, kc * DM:(kc + 1) * DM],
                          wqT[kc * 128:(kc + 1) * 128, :], inc="b_wld")
                    P.dma("gpsimd", wk_sb[:, kc * KV * HD:(kc + 1) * KV * HD],
                          wkT[kc * 128:(kc + 1) * 128, :], inc="b_wld")
                    P.dma("gpsimd", wv_sb[:, kc * KV * HD:(kc + 1) * KV * HD],
                          wvT[kc * 128:(kc + 1) * 128, :], inc="b_wld")
                b_wld_all = P.val("b_wld")

                def w1_mean_slice(i):
                    rc, qt = i // 4, i % 4
                    pb = i % 2
                    w = [("w1ab_done", max(0, P.val("w1ab_done") - 1))] if i >= 2 else []
                    ld = P.dma("gpsimd", w1p[pb][:],
                               w1T[rc * 128:(rc + 1) * 128, qt * DM:(qt + 1) * DM],
                               waits=w, inc="w1_ld%d" % pb)
                    P.emit("scalar", lambda e, pb=pb, i=i: e.activation(
                        w1p[pb][:], w1p[pb][:], AF.Abs,
                        accum_out=wsum2[:, i:i + 1]), waits=[("w1_ld%d" % pb, ld)],
                        inc="w1ab_done")

                def rope(src, width, nheads, gb, tl, out_bf, tag, w_first, freetag):
                    hw = 128
                    cs1 = cosb[gb][:, tl * 128:(tl + 1) * 128].unsqueeze(1).broadcast_to([128, nheads, hw])
                    sn1 = sinb[gb][:, tl * 128:(tl + 1) * 128].unsqueeze(1).broadcast_to([128, nheads, hw])
                    cs2 = cosb[gb][:, tl * 128:(tl + 1) * 128].unsqueeze(1).unsqueeze(1).broadcast_to([128, nheads, 2, hw])
                    x4 = src.rearrange("p (h two j) -> p h two j", two=2, j=hw)
                    t14 = t1q[:, :width].rearrange("p (h two j) -> p h two j", two=2, j=hw)
                    t24 = t2q[:, :width].rearrange("p (h two j) -> p h two j", two=2, j=hw)
                    o4 = out_bf.rearrange("p (h two j) -> p h two j", two=2, j=hw)
                    P.emit("vector", lambda e: e.tensor_tensor(t14[:], x4[:], cs2, ALU.mult),
                           waits=w_first)
                    P.emit("vector", lambda e: e.tensor_tensor(
                        t24[:, :, 0, :], x4[:, :, 1, :], sn1, ALU.mult))
                    P.emit("vector", lambda e: e.tensor_tensor(
                        t24[:, :, 1, :], x4[:, :, 0, :], sn1, ALU.mult),
                        inc=freetag)
                    P.emit("vector", lambda e: e.tensor_tensor(
                        o4[:, :, 0, :], t14[:, :, 0, :], t24[:, :, 0, :], ALU.subtract))
                    return P.emit("vector", lambda e: e.tensor_tensor(
                        o4[:, :, 1, :], t14[:, :, 1, :], t24[:, :, 1, :], ALU.add),
                        inc=tag + "rp")

                def post_ln(x_bf, off, nseg, q_heads, w_first, apply_waits):
                    tag = "bm"
                    xv = x_bf[:, off:off + nseg * HD]
                    x3 = xv.rearrange("p (h d) -> p h d", d=HD)
                    s_sum = stq[:, 0:nseg]
                    s_ssq = stq[:, 16:16 + nseg]
                    s_mu = stq[:, 32:32 + nseg]
                    s_var = stq[:, 48:48 + nseg]
                    P.emit("vector", lambda e: e.tensor_reduce(
                        s_sum.unsqueeze(2), x3[:], AX.X, ALU.add),
                        waits=list(w_first) + [(tag + "sd", P.val(tag + "sd"))])
                    sqd = None
                    for hh in range(nseg):
                        sqd = P.emit("scalar", lambda e, hh=hh: e.activation(
                            sqsm[:], xv[:, hh * HD:(hh + 1) * HD], AF.Square,
                            accum_out=s_ssq[:, hh:hh + 1]),
                            waits=(list(w_first) + [(tag + "v", P.val(tag + "v"))]
                                   if hh == 0 else []),
                            inc=tag + "sq")
                    P.emit("vector", lambda e: e.tensor_scalar(
                        s_mu, s_sum, 1.0 / HD, None, ALU.mult),
                        waits=[(tag + "sq", sqd)])
                    P.emit("vector", lambda e: e.tensor_scalar(
                        s_var, s_ssq, 1.0 / HD, None, ALU.mult))
                    P.emit("vector", lambda e: e.scalar_tensor_tensor(
                        s_sum, s_mu, 1.0, s_mu, ALU.mult, ALU.mult))
                    vd = P.emit("vector", lambda e: e.tensor_tensor(
                        s_var, s_var, s_sum, ALU.subtract), inc=tag + "v")
                    sdd = P.emit("scalar", lambda e: e.activation(
                        s_sum, s_var, AF.Sqrt, bias=cc_(IC_EPS6)),
                        waits=[(tag + "v", vd)], inc=tag + "sd")
                    P.emit("vector", lambda e: e.reciprocal(s_var, s_sum),
                           waits=[(tag + "sd", sdd)])
                    if q_heads:
                        P.emit("vector", lambda e: e.tensor_scalar(
                            s_var[:, 0:q_heads], s_var[:, 0:q_heads], 256.0, None, ALU.mult))
                    P.emit("vector", lambda e: e.tensor_tensor(
                        s_mu, s_mu, s_var, ALU.mult))
                    rs_b = s_var.unsqueeze(2).broadcast_to([128, nseg, HD])
                    b_b = s_mu.unsqueeze(2).broadcast_to([128, nseg, HD])
                    ov = qkvo[:, off:off + nseg * HD].rearrange("p (h d) -> p h d", d=HD)
                    t3 = t1q[:, :nseg * HD].rearrange("p (h d) -> p h d", d=HD)
                    P.emit("vector", lambda e: e.tensor_tensor(t3[:], x3[:], rs_b, ALU.mult),
                           waits=apply_waits)
                    return P.emit("vector", lambda e: e.tensor_tensor(
                        ov[:], t3[:], b_b, ALU.subtract), inc=tag + "ap")

                emit_1a_chunk = _ph_1a()
                for cc4 in range(4):
                    emit_1a_chunk(cc4)
                for g in range(NG):
                    if g + 1 < NG:
                        for cc4 in range(4):
                            emit_1a_chunk((g + 1) * GCH + cc4)
                    gb = 0
                    wld = [("b_gfree", P.val("b_gfree"))] if g >= 1 else []
                    rb2 = g % 2
                    P.dma("sync", cosb[rb2].rearrange("p (c j) -> p c j", j=128),
                          ropec[g * GTOK:(g + 1) * GTOK, :].rearrange("(c p) j -> p c j", p=128),
                          waits=(wld + ([("bqap", P.val("bqap"))] if g >= 2 else [])),
                          inc="b_rld")
                    P.dma("sync", sinb[rb2].rearrange("p (c j) -> p c j", j=128),
                          ropes[g * GTOK:(g + 1) * GTOK, :].rearrange("(c p) j -> p c j", p=128),
                          inc="b_rld")
                    for kc in range(16):
                        P.dma_t("sync", xnf[gb][:, kc * GTOK:(kc + 1) * GTOK],
                                xn_hbm[g * GTOK:(g + 1) * GTOK, kc * 128:(kc + 1) * 128],
                                waits=([("a_st%d" % g, 64)] if kc == 0 else []),
                                inc="b_xn")
                    b_xn_need = P.val("b_xn")
                    b_rld_need = P.val("b_rld")
                    for tl in range(GCH):
                        cc = g * GCH + tl
                        for k4 in range(4):
                            i_w1 = cc * 4 + k4
                            if i_w1 < 64:
                                w1_mean_slice(i_w1)
                        own = cc >= 4
                        tco = cc - 4
                        ob = cc % 2

                        wl = [("b_wld", b_wld_all), ("b_xn", b_xn_need),
                              ("b_rld", b_rld_need)]
                        if P.val("b_kev") >= 2:
                            wl += [("b_kev", P.val("b_kev") - 1),
                                   ("b_vev", P.val("b_vev") - 1)]
                        wq_mm = [("b_qev", P.val("b_qev"))] if P.val("b_qev") > 0 else []
                        mms = []
                        for kc in range(16):
                            lhs = xnf[gb][:, kc * GTOK + tl * 128: kc * GTOK + (tl + 1) * 128]
                            mms.append((ps_k[ob][:, :], lhs,
                                        wk_sb[:, kc * 512:(kc + 1) * 512], kc == 0, kc == 15, None))
                        for kc in range(16):
                            lhs = xnf[gb][:, kc * GTOK + tl * 128: kc * GTOK + (tl + 1) * 128]
                            mms.append((ps_v[ob][:, :], lhs,
                                        wv_sb[:, kc * 512:(kc + 1) * 512], kc == 0, kc == 15, None))
                        if own:
                            firstq = True
                            for kc in range(16):
                                lhs = xnf[gb][:, kc * GTOK + tl * 128: kc * GTOK + (tl + 1) * 128]
                                for nb in range(4):
                                    mms.append((ps_q[:, nb * 512:(nb + 1) * 512], lhs,
                                                wq_sb[:, kc * DM + nb * 512: kc * DM + (nb + 1) * 512],
                                                kc == 0, kc == 15,
                                                (wq_mm if firstq else None)))
                                    firstq = False
                        for n_, (o, l, r, st_, sp, xw) in enumerate(mms):
                            ww = list(wl) if n_ == 0 else []
                            if xw:
                                ww += xw
                            mmd = P.emit("tensor", lambda e, o=o, l=l, r=r, st_=st_, sp=sp:
                                         e.matmul(o, l, r, start=st_, stop=sp),
                                         waits=ww,
                                         inc=("b_mm" if n_ == len(mms) - 1 else None))

                        wev = [("b_mm", mmd)]
                        if cc >= 1:
                            wev += [("bkfree", P.val("bkfree")),
                                    ("bqfree", P.val("bqfree")),
                                    ("bmap", P.val("bmap"))]
                        kev = P.emit("scalar", lambda e, ob=ob: e.copy(kf[0][:], ps_k[ob][:]),
                                     waits=wev, inc="b_kev")
                        vev = P.emit("scalar", lambda e, ob=ob: e.copy(
                            qkvp[:, DM + KV * HD:], ps_v[ob][:]), inc="b_vev")
                        wnop = [("b_kev", kev), ("b_vev", vev)]
                        if own:
                            qev = P.emit("scalar", lambda e: e.copy(qf[0][:], ps_q[:]),
                                         inc="b_qev")
                            wnop.append(("b_qev", qev))
                        P.emit("scalar", lambda e: e.nop(), nodrain=True,
                               waits=wnop, inc="b_ev")

                        wo = [("b_ost", P.val("b_ost"))] if cc >= 1 else []
                        rpk = rope(kf[0][:], KV * HD, KV, rb2, tl,
                                   qkvp[:, DM:DM + KV * HD], "bk",
                                   [("b_kev", kev)], "bkfree")
                        if own:
                            rpq = rope(qf[0][:], DM, H, rb2, tl, qkvp[:, 0:DM],
                                       "bq", [("b_qev", qev)], "bqfree")
                            lw = [("bkrp", rpk), ("bqrp", rpq), ("b_vev", vev)]
                            ap = post_ln(qkvp, 0, 12, 8, lw, wo)
                        else:
                            P.emit("vector", lambda e: e.nop(), nodrain=True, inc="bqfree")
                            lw = [("bkrp", rpk), ("b_vev", vev)]
                            ap = post_ln(qkvp, DM, 4, 0, lw, wo)
                        P.emit("vector", lambda e: e.nop(), nodrain=True,
                               waits=[("bmap", ap)], inc="bmap")

                        P.dma("sync", k_hbm[cc * 128:(cc + 1) * 128, :],
                              qkvo[:, DM:DM + KV * HD],
                              waits=[("bmap", P.val("bmap"))], inc="b_ost")
                        P.dma("sync", v_hbm[cc * 128:(cc + 1) * 128, :],
                              qkvo[:, DM + KV * HD:], inc="b_ost")
                        if own:
                            P.dma("sync", q_hbm[tco * 128:(tco + 1) * 128, :],
                                  qkvo[:, 0:DM], inc="b_ost")
                    P.emit("tensor", lambda e: e.nop(),
                           waits=[("b_mm", P.val("b_mm"))], inc="b_gfree")

                P.emit("vector", lambda e: e.reduce_sum(
                    bc_sc[:, 0:1].unsqueeze(2),
                    wsum2[:, 0:64].unsqueeze(1), AX.X),
                    waits=[("w1ab_done", P.val("w1ab_done"))], inc="w1s")
                m1 = P.emit("tensor", lambda e: e.matmul(
                    ps_bc[0:1, 0:1], onesf_sb[:, 0:1], bc_sc[:, 0:1],
                    start=True, stop=True),
                    waits=[("w1s", P.val("w1s")), ("b_qev", P.val("b_qev"))],
                    inc="w1s_mm")
                e1 = P.emit("scalar", lambda e: e.copy(bc_sc[0:1, 1:2], ps_bc[0:1, 0:1]),
                            waits=[("w1s_mm", m1)], inc="w1s_ev")
                m2 = P.emit("tensor", lambda e: e.matmul(
                    ps_bc[:, 0:1], onesf_sb[0:1, :], bc_sc[0:1, 1:2],
                    start=True, stop=True), waits=[("w1s_ev", e1)], inc="w1s_mm")
                e2 = P.emit("scalar", lambda e: e.copy(w1m_col[:], ps_bc[:, 0:1]),
                            waits=[("w1s_mm", m2)], inc="w1s_ev")
                P.emit("vector", lambda e: e.tensor_scalar(
                    w1m_col[:], w1m_col[:], 1.0 / (DM * HID), 1e-5, ALU.mult, ALU.max),
                    waits=[("w1s_ev", e2)])
                P.emit("vector", lambda e: e.reciprocal(w1inv[:], w1m_col[:]),
                       inc="w1m_done")
            P.barrier("p1b")

        def _ph_1c():
            with contextlib.ExitStack() as ph:
                phase_reset()
                wo_sb = A16.take(16 * DM)
                mask_sb = A16.take(8 * GQ)
                qfm = [A16.take(16 * GQ) for i in range(2)]
                kfm = A16.take(4 * 1024)
                vtm = A16.take(8 * 512)
                pt_sb = A16.take(8 * GQ)
                th_s = A32.take(2 * GQ)
                u_s = A32.take(2 * GQ)
                z_a = A32.take(GQ)
                varc = A32.take(16)
                tmpc2 = A32.take(16)
                afm = A16.take(16 * GQ)
                xres = [A32.take(DM)]
                x1t = A32.take(DM)
                xn2t = A32.take(DM)
                sqsm2 = A32.take(DM // 2)
                scrA = A32.take(DM // 2)
                scrB = A32.take(DM // 2)
                zi32 = AI.take(DM // 2)
                xqv = A16.take(DM)
                st2 = A32.take(16)
                tmpc = A32.take(2)
                ps_st = [ph.enter_context(nc.psum_tensor(f"ps_st{i}", [128, 2 * GQ], F32)) for i in range(2)]
                ps_z = ph.enter_context(nc.psum_tensor("ps_z", [128, GQ], F32))
                ps_pv = [ph.enter_context(nc.psum_tensor(f"ps_pv{i}", [128, GQ], F32)) for i in range(2)]
                ps_o = ph.enter_context(nc.psum_tensor("ps_o", [128, 512], F32))

                P.dma("sync", mask_sb.rearrange("p (t q) -> p t q", q=GQ),
                      masks[8:16, :, :].rearrange("t p q -> p t q"), inc="c_mld")
                for kc in range(16):
                    P.dma("gpsimd", wo_sb[:, kc * DM:(kc + 1) * DM],
                          woT[kc * 128:(kc + 1) * 128, :], inc="c_wld")
                c_wld_all = P.val("c_wld")
                c_mld_all = P.val("c_mld")

                qfm_free = [0, 0]
                pend_tasks = []

                def ln2_tasks(tc, x1st_need):
                    def t0():
                        ld = P.dma(
                            "sync", x1t[:], x1_hbm[tc * 128:(tc + 1) * 128, :],
                            waits=[("c_x1st", x1st_need)], inc="c_xrld")
                        ln_stats(x1t[:], DM, st2, sqsm2, cc_(IC_EPS5), "p",
                                 [("c_xrld", ld)], 1.0 / DM,
                                 defer_var=varc[:, tc:tc + 1])

                    def t1():
                        wap2 = [("c_q", P.val("c_q"))] if P.val("c_q") > 0 else []
                        P.emit("vector", lambda e: e.tensor_scalar(
                            xn2t[:], x1t[:], st2[:, 2:3], None,
                            ALU.subtract), waits=wap2, inc="c_x1done")
                        P.emit("vector", lambda e: e.tensor_reduce(
                            amax2c[:, tc:tc + 1].unsqueeze(2),
                            xn2t[:, :].unsqueeze(1), AX.X, ALU.max,
                            apply_absolute_value=True))
                        P.emit("vector", lambda e: e.reciprocal(
                            tmpc[:, 0:1], amax2c[:, tc:tc + 1]))
                        P.emit("vector", lambda e: e.tensor_scalar(
                            s2col[:, tc:tc + 1], tmpc[:, 0:1], 127.0, None, ALU.mult))

                    def mk_q(pc):
                        def tq():
                            cs_ = slice(pc * (DM // 2), (pc + 1) * (DM // 2))
                            wq_ = ([("c_xqst", P.val("c_xqst"))]
                                   if (tc >= 1 and pc == 0) else [])
                            P.emit("vector", lambda e: e.tensor_scalar(
                                scrA[:], xn2t[:, cs_], s2col[:, tc:tc + 1],
                                None, ALU.mult), waits=wq_)
                            P.emit("vector", lambda e: e.tensor_copy(zi32[:], scrA[:]))
                            P.emit("vector", lambda e: e.scalar_tensor_tensor(
                                scrB[:], scrA[:], 0.5, zi32[:], ALU.subtract, ALU.is_ge))
                            P.emit("vector", lambda e: e.scalar_tensor_tensor(
                                xn2t[:, cs_], scrA[:], 0.5, zi32[:], ALU.add, ALU.is_le))
                            P.emit("vector", lambda e: e.tensor_tensor(
                                scrA[:], zi32[:], scrB[:], ALU.add))
                            qd = P.emit("vector", lambda e: e.tensor_tensor(
                                xqv[:, cs_], scrA[:], xn2t[:, cs_], ALU.subtract),
                                inc="c_q")
                            if pc == 1:
                                P.dma("sync", xq_hbm[tc * 128:(tc + 1) * 128, :],
                                      xqv[:], waits=[("c_q", qd)], inc="c_xqst")
                        return tq

                    return [t0, t1, mk_q(0), mk_q(1)]

                for qg in range(NQG):
                    qb = qg % 2
                    wq_ = [("c_qfm_free", qfm_free[qb])] if qg >= 2 else []
                    for kc in range(16):
                        P.dma_t("sync", qfm[qb][:, kc * GQ:(kc + 1) * GQ],
                                q_hbm[qg * GQ:(qg + 1) * GQ, kc * 128:(kc + 1) * 128],
                                waits=(wq_ if kc == 0 else []), inc="c_qld")
                    wk_ = [("c_kv_free", P.val("c_kv_free"))] if qg >= 1 else []
                    for kc in range(4):
                        P.dma_t("sync", kfm[:, kc * 1024:(kc + 1) * 1024],
                                k_hbm[qg * GQ:qg * GQ + 1024, kc * 128:(kc + 1) * 128],
                                waits=(wk_ if kc == 0 else []), inc="c_kld")
                    P.dma("sync", vtm.rearrange("p (c d) -> p c d", d=KV * HD),
                          v_hbm[qg * GQ:qg * GQ + 1024, :].rearrange("(c p) d -> p c d", p=128),
                          inc="c_kld")
                    c_qld_need = P.val("c_qld")
                    c_kld_need = P.val("c_kld")

                    steps = [(h, ktp) for h in range(H) for ktp in range(4)]

                    def emit_smm(h, ktp):
                        kv = h // 4
                        sb = ktp % 2
                        wS = [("c_qld", c_qld_need), ("c_kld", c_kld_need),
                              ("c_wld", c_wld_all), ("c_mld", c_mld_all)]
                        if P.val("c_tanh") >= 2:
                            wS.append(("c_tanh", P.val("c_tanh") - 1))
                        first = True
                        smm = None
                        for half in range(2):
                            kt = 2 * ktp + half
                            for dc in range(2):
                                kc4 = kv * 2 + dc
                                lhs = kfm[:, kc4 * 1024 + kt * 128: kc4 * 1024 + (kt + 1) * 128]
                                rhs = qfm[qb][:, (2 * h + dc) * GQ:(2 * h + dc + 1) * GQ]
                                smm = P.emit("tensor", lambda e, l=lhs, r=rhs, sb=sb, dc=dc, half=half:
                                             e.matmul(ps_st[sb][:, half * GQ:(half + 1) * GQ],
                                                      l, r, start=(dc == 0), stop=(dc == 1)),
                                             waits=(wS if first else []),
                                             inc=("c_smm" if (half == 1 and dc == 1) else None))
                                first = False
                        return smm

                    smm_next = emit_smm(*steps[0])
                    for idx, (h, ktp) in enumerate(steps):
                        kv = h // 4
                        sb = ktp % 2
                        smm = smm_next
                        if True:
                            wT = [("c_smm", smm)]
                            if P.val("c_mask") >= 2:
                                wT.append(("c_mask", P.val("c_mask") - 1))
                            tnh = P.emit("scalar", lambda e, sb=sb: e.activation(
                                th_s[:], ps_st[sb][:], AF.Tanh, scale=cc_(IC_1480)),
                                waits=wT, inc="c_tanh")
                            wM = [("c_tanh", tnh)]
                            if P.val("c_exp") >= 2:
                                wM.append(("c_exp", P.val("c_exp") - 1))
                            msk = P.emit("vector", lambda e, ktp=ktp: e.tensor_tensor(
                                u_s[:], th_s[:],
                                mask_sb[:, 2 * ktp * GQ:(2 * ktp + 2) * GQ],
                                ALU.add), waits=wM, inc="c_mask")
                            wE = [("c_mask", msk)]
                            if P.val("c_ptfree") >= 3:
                                wE.append(("c_ptfree", P.val("c_ptfree") - 3))
                            ex = P.emit("scalar", lambda e, ktp=ktp: e.activation(
                                pt_sb[:, 2 * ktp * GQ:(2 * ktp + 2) * GQ], u_s[:], AF.Exp,
                                scale=cc_(IC_30), bias=cc_(IC_M30)),
                                waits=wE, inc="c_exp")
                            if idx + 1 < len(steps):
                                smm_next = emit_smm(*steps[idx + 1])
                            for half in range(2):
                                kt = 2 * ktp + half
                                P.emit("tensor", lambda e, kt=kt: e.matmul(
                                    ps_z[:], ones_sb[:], pt_sb[:, kt * GQ:(kt + 1) * GQ],
                                    start=(kt == 0), stop=(kt == 7)),
                                    waits=[("c_exp", ex), ("c_zfree", P.val("c_zfree"))] if kt == 0 else ([("c_exp", ex)] if half == 0 else []))
                                for pvh in range(2):
                                    lhsv = vtm[:, kt * 512 + kv * 256 + pvh * 128:
                                               kt * 512 + kv * 256 + (pvh + 1) * 128]
                                    P.emit("tensor", lambda e, l=lhsv, pvh=pvh, kt=kt: e.matmul(
                                        ps_pv[pvh][:], l, pt_sb[:, kt * GQ:(kt + 1) * GQ],
                                        start=(kt == 0), stop=(kt == 7)),
                                        waits=([("c_pvfree", P.val("c_pvfree"))] if (kt == 0 and pvh == 0) else []),
                                        inc=("c_ptfree" if (half == 1 and pvh == 1) else None))
                        if ktp == 3:
                            zpv = P.emit("tensor", lambda e: e.nop(),
                                         waits=[("c_ptfree", P.val("c_ptfree"))], inc="c_zpv")
                            P.emit("vector", lambda e: e.reciprocal(z_a[:], ps_z[:]),
                                   waits=[("c_zpv", zpv)], inc="c_zfree")
                            wafm = []
                            if qg > 0:
                                wafm.append(("c_oproj", P.val("c_oproj")))
                            for half in range(2):
                                P.emit("vector", lambda e, h=h, half=half: e.scalar_tensor_tensor(
                                    afm[:, (2 * h + half) * GQ:(2 * h + half + 1) * GQ],
                                    ps_pv[half][:], 1.0, z_a[:], ALU.mult, ALU.mult),
                                    waits=(wafm if half == 0 else []),
                                    inc=("c_pvfree" if half == 1 else None))
                            P.emit("vector", lambda e: e.nop(), nodrain=True,
                                   waits=[("c_pvfree", P.val("c_pvfree"))], inc="c_afm")
                        if pend_tasks:
                            pend_tasks.pop(0)()
                    c_afm_all = P.val("c_afm")

                    for tl in range(4):
                        tc = qg * 4 + tl
                        rb = 0
                        wr = [("c_xfree", P.val("c_xfree"))] if tc >= 1 else []
                        xl = P.dma("sync", xres[rb][:],
                                   x_tm[HALO + tc * 128:HALO + (tc + 1) * 128, :],
                                   waits=wr, inc="c_xld")
                        for h2 in range(4):
                            om = None
                            for kc in range(16):
                                lhs = afm[:, kc * GQ + tl * 128: kc * GQ + (tl + 1) * 128]
                                rhs = wo_sb[:, kc * DM + h2 * 512: kc * DM + (h2 + 1) * 512]
                                om = P.emit("tensor", lambda e, l=lhs, r=rhs, kc=kc:
                                            e.matmul(ps_o[:], l, r,
                                                     start=(kc == 0), stop=(kc == 15)),
                                            waits=([("c_afm", c_afm_all),
                                                    ("c_oev", P.val("c_oev"))]
                                                   if kc == 0 else []),
                                            inc=("c_omm" if kc == 15 else None))
                            wadd = [("c_omm", om), ("c_xld", xl),
                                    ("c_x1st", P.val("c_x1st"))]
                            if tl == 0 and h2 == 0 and P.val("c_q") > 0:
                                wadd.append(("c_q", P.val("c_q")))
                            P.emit("vector", lambda e, h2=h2, rb=rb: e.tensor_tensor(
                                x1t[:, h2 * 512:(h2 + 1) * 512], ps_o[:],
                                xres[rb][:, h2 * 512:(h2 + 1) * 512], ALU.add),
                                waits=wadd, inc="c_oev")
                        xd = P.emit("vector", lambda e: e.nop(), nodrain=True,
                                    waits=[("c_oev", P.val("c_oev"))], inc="c_xfree")
                        P.dma("sync", x1_hbm[tc * 128:(tc + 1) * 128, :], x1t[:],
                              waits=[("c_oev", P.val("c_oev"))], inc="c_x1st")
                    for tl in range(4):
                        pend_tasks.extend(ln2_tasks(qg * 4 + tl, P.val("c_x1st")))
                    qfm_free[qb] = P.emit(
                        "tensor", lambda e: e.nop(),
                        waits=[("c_smm", P.val("c_smm"))], inc="c_qfm_free")
                    P.emit("tensor", lambda e: e.nop(),
                           waits=[("c_omm", P.val("c_omm"))], inc="c_oproj")
                    P.emit("tensor", lambda e: e.nop(),
                           waits=[("c_ptfree", P.val("c_ptfree"))], inc="c_kv_free")
                    if qg == 0:
                        P.dma("sync", mask_sb.rearrange("p (t q) -> p t q", q=GQ),
                              masks[0:8, :, :].rearrange("t p q -> p t q"),
                              waits=[("c_exp", P.val("c_exp"))], inc="c_mld")
                        c_mld_all = P.val("c_mld")
                while pend_tasks:
                    pend_tasks.pop(0)()
                sdd = P.emit("scalar", lambda e: e.activation(
                    varc[:], varc[:], AF.Sqrt, bias=cc_(IC_EPS5)),
                    waits=[("pv", P.val("pv"))], inc="psd")
                P.emit("vector", lambda e: e.reciprocal(tmpc2[:], varc[:]),
                       waits=[("psd", sdd)])
                P.emit("vector", lambda e: e.tensor_tensor(
                    amax2c[:, 0:NT], amax2c[:, 0:NT], tmpc2[:], ALU.mult))
                P.emit("vector", lambda e: e.tensor_scalar(
                    amax2c[:, 0:NT], amax2c[:, 0:NT], 1e-5, None, ALU.max))
            P.barrier("p1c")

        def _ph_2():
            NGRP = 8
            GH = HID // NGRP
            with contextlib.ExitStack() as ph:
                phase_reset()
                xqfm = A16.take(16 * DM)
                w1p = [A32.take(16 * 128) for i in range(2)]
                w1q = [A16.take(16 * GH) for i in range(2)]
                sa_s = A16.take(16 * 128)
                sb_s = A16.take(16 * 128)
                hy = A32.take(GH)
                ht = [A32.take(GH) for i in range(2)]
                w2p = [A32.take(DM) for i in range(2)]
                w2a = A16.take(DM)
                w2b = A16.take(DM)
                w2v = [A16.take(DM) for i in range(1)]
                tmp2 = A32.take(2)
                amaxg = A32.take(128)
                ps_f = [ph.enter_context(nc.psum_tensor(f"ps_f{i}", [128, GH], F32)) for i in range(2)]
                ps_bc = ph.enter_context(nc.psum_tensor("ps_bc2", [128, 1], F32))

                for kc in range(16):
                    P.dma_t("sync", xqfm[:, kc * DM:(kc + 1) * DM],
                            xq_hbm[:, kc * 128:(kc + 1) * 128], inc="f_xq")
                f_xq_all = P.val("f_xq")
                for tc in range(NT):
                    P.emit("vector", lambda e, tc=tc: e.scalar_tensor_tensor(
                        fac2[:, tc:tc + 1], amax2c[:, tc:tc + 1], 0.5 / 127.0,
                        w1m_col[:], ALU.mult, ALU.mult),
                        waits=([("w1m_done", P.val("w1m_done"))] if tc == 0 else []),
                        inc="f_fac")
                f_fac_all = P.val("f_fac")

                def w2_mean_slice(rc):
                    pb = rc % 2
                    w = [("w2ab_done", max(0, P.val("w2ab_done") - 1))] if rc >= 2 else []
                    ld = P.dma("gpsimd", w2p[pb][:], w2T[rc * 128:(rc + 1) * 128, :],
                               waits=w, inc="w2_ld%d" % pb)
                    P.emit("scalar", lambda e, pb=pb, rc=rc: e.activation(
                        w2p[pb][:], w2p[pb][:], AF.Abs,
                        accum_out=wsum2[:, rc:rc + 1]), waits=[("w2_ld%d" % pb, ld)],
                        inc="w2ab_done")

                def w2_mean_finish():
                    P.emit("vector", lambda e: e.reduce_sum(
                        tmp2[:, 0:1].unsqueeze(2),
                        wsum2[:, :].unsqueeze(1), AX.X),
                        waits=[("w2ab_done", P.val("w2ab_done"))], inc="w2s")
                    m1 = P.emit("tensor", lambda e: e.matmul(
                        ps_bc[0:1, 0:1], onesf_sb[:, 0:1], tmp2[:, 0:1],
                        start=True, stop=True), waits=[("w2s", P.val("w2s"))], inc="w2mm")
                    e1 = P.emit("scalar", lambda e: e.copy(tmp2[0:1, 1:2], ps_bc[0:1, 0:1]),
                                waits=[("w2mm", m1)], inc="w2ev")
                    m2 = P.emit("tensor", lambda e: e.matmul(
                        ps_bc[:, 0:1], onesf_sb[0:1, :], tmp2[0:1, 1:2],
                        start=True, stop=True), waits=[("w2ev", e1)], inc="w2mm")
                    e2 = P.emit("scalar", lambda e: e.copy(w2m_col[:], ps_bc[:, 0:1]),
                                waits=[("w2mm", m2)], inc="w2ev")
                    P.emit("vector", lambda e: e.tensor_scalar(
                        w2m_col[:], w2m_col[:], 1.0 / (DM * HID), 1e-5, ALU.mult, ALU.max),
                        waits=[("w2ev", e2)])
                    P.emit("vector", lambda e: e.reciprocal(w2inv[:], w2m_col[:]),
                           inc="w2m_done")

                def w2_quant_slice(rc):
                    pb = rc % 2
                    w = [("w2q_st", P.val("w2q_st"))] if rc >= 2 else \
                        [("w2ab_done", P.val("w2ab_done"))]
                    ld = P.dma("gpsimd", w2p[pb][:], w2T[rc * 128:(rc + 1) * 128, :],
                               waits=w, inc="w2_ld2%d" % pb)
                    P.emit("vector", lambda e, pb=pb: e.tensor_scalar(
                        w2a[:], w2p[pb][:], w2inv[:], 0.5, ALU.mult, ALU.is_ge),
                        waits=[("w2_ld2%d" % pb, ld), ("w2m_done", P.val("w2m_done"))])
                    P.emit("vector", lambda e, pb=pb: e.tensor_scalar(
                        w2b[:], w2p[pb][:], w2inv[:], -0.5, ALU.mult, ALU.is_ge))
                    vq = P.emit("vector", lambda e: e.scalar_tensor_tensor(
                        w2v[0][:], w2a[:], -1.0, w2b[:], ALU.add, ALU.add),
                        waits=[("w2q_st", P.val("w2q_st"))], inc="w2_vq")
                    P.dma("sync", w2q_hbm[rc * 128:(rc + 1) * 128, :], w2v[0][:],
                          waits=[("w2_vq", vq)], inc="w2q_st")

                def w1_piece(g, pi):
                    gb = g % 2
                    idx = g * 8 + pi
                    pb = idx % 2
                    w = [("f_w1q", P.val("f_w1q"))] if idx >= 2 else []
                    ld = P.dma("gpsimd",
                               w1p[pb].rearrange("p (kc j) -> p kc j", j=128),
                               w1T[:, g * GH + pi * 128: g * GH + (pi + 1) * 128]
                               .rearrange("(kc p) j -> p kc j", p=128),
                               waits=w, inc="f_w1ld%d" % pb)
                    wtq = [("f_w1ld%d" % pb, ld), ("w1m_done", P.val("w1m_done"))]
                    if idx >= 2:
                        wtq.append(("f_sg", 2 * idx - 2))
                    tqd = P.emit("vector", lambda e, pb=pb: e.tensor_scalar(
                        w1p[pb][:], w1p[pb][:], w1inv[:], None, ALU.mult),
                        waits=wtq, inc="f_tq")
                    sad = P.emit("scalar", lambda e, pb=pb: e.activation(
                        sa_s[:], w1p[pb][:], AF.Sign, bias=cc_(IC_M05)),
                        waits=[("f_tq", tqd), ("f_w1q", P.val("f_w1q"))],
                        inc="f_sg")
                    sbd = P.emit("scalar", lambda e, pb=pb: e.activation(
                        sb_s[:], w1p[pb][:], AF.Sign, bias=cc_(IC_P05)), inc="f_sg")
                    wmm = [("f_mm_g", P.val("f_mm_g"))] if g >= 2 else []
                    P.emit("vector", lambda e, gb=gb, pi=pi: e.tensor_tensor(
                        w1q[gb].rearrange("p (kc j) -> p kc j", j=GH)[:, :, pi * 128:(pi + 1) * 128],
                        sa_s.rearrange("p (kc j) -> p kc j", j=128),
                        sb_s.rearrange("p (kc j) -> p kc j", j=128), ALU.add),
                        waits=wmm + [("f_sg", sbd)], inc="f_w1q")

                amax_init = [False] * NT
                for pi in range(8):
                    w1_piece(0, pi)
                for g in range(NGRP):
                    gb = g % 2
                    f_w1q_need = P.val("f_w1q")

                    for tc in range(NT):
                        if tc % 2 == 0 and g + 1 < NGRP:
                            w1_piece(g + 1, tc // 2)
                        slot = g * NT + tc
                        if slot < 64:
                            w2_mean_slice(slot)
                        elif slot == 64:
                            w2_mean_finish()
                        elif slot - 65 < 64:
                            w2_quant_slice(slot - 65)
                        tb = tc % 2
                        wl = [("f_w1q", f_w1q_need), ("f_xq", f_xq_all)]
                        if P.val("f_ev") >= 2:
                            wl.append(("f_ev", P.val("f_ev") - 1))
                        first = True
                        for kc in range(16):
                            lhs = xqfm[:, kc * DM + tc * 128: kc * DM + (tc + 1) * 128]
                            for nb in range(2):
                                rhs = w1q[gb][:, kc * GH + nb * 512: kc * GH + (nb + 1) * 512]
                                fmm = P.emit("tensor", lambda e, l=lhs, r=rhs, nb=nb, kc=kc, tb=tb:
                                             e.matmul(ps_f[tb][:, nb * 512:(nb + 1) * 512],
                                                      l, r, start=(kc == 0), stop=(kc == 15)),
                                             waits=(wl if first else []),
                                             inc=("f_mm" if (kc == 15 and nb == 1) else None))
                                first = False
                        wep = [("f_mm", fmm), ("f_fac", f_fac_all)]
                        whst = [("f_hst", P.val("f_hst"))] if P.val("f_hst") > 0 else []
                        P.emit("scalar", lambda e, tb=tb, tc=tc: e.activation(
                            hy[:], ps_f[tb][:], AF.Relu, scale=fac2[:, tc:tc + 1]),
                            waits=wep, inc="f_ev")
                        hb = tc % 2
                        hd_ = P.emit("scalar", lambda e, hb=hb: e.activation(
                            ht[hb][:], hy[:], AF.Square),
                            waits=whst, inc="f_h")
                        P.emit("vector", lambda e, tc=tc, hb=hb, g=g: e.tensor_reduce(
                            amaxg[:, g * NT + tc: g * NT + tc + 1].unsqueeze(2),
                            ht[hb][:, :].unsqueeze(1), AX.X, ALU.max),
                            waits=[("f_h", hd_)])
                        P.dma("sync", h_hbm[tc * 128:(tc + 1) * 128, g * GH:(g + 1) * GH],
                              ht[hb][:], waits=[("f_h", hd_)], inc="f_hst")
                        if g == NGRP - 1:
                            P.emit("vector", lambda e, tc=tc: e.tensor_reduce(
                                amax3r[:, tc:tc + 1].unsqueeze(2),
                                amaxg.rearrange("p (g t) -> p t g", t=NT)[:, tc:tc + 1, :],
                                AX.X, ALU.max))
                            P.emit("vector", lambda e, tc=tc: e.tensor_scalar(
                                amax3c[:, tc:tc + 1], amax3r[:, tc:tc + 1], 1e-5, None, ALU.max))
                            P.emit("vector", lambda e, tc=tc: e.reciprocal(
                                tmp2[:, 1:2], amax3c[:, tc:tc + 1]))
                            P.emit("vector", lambda e, tc=tc: e.tensor_scalar(
                                s3col[:, tc:tc + 1], tmp2[:, 1:2], 127.0, None, ALU.mult))
                            P.emit("vector", lambda e, tc=tc: e.scalar_tensor_tensor(
                                fac3[:, tc:tc + 1], amax3c[:, tc:tc + 1], 1.0 / 127.0,
                                w2m_col[:], ALU.mult, ALU.mult),
                                waits=[("w2m_done", P.val("w2m_done"))], inc="f_s3")
                    P.emit("tensor", lambda e: e.nop(),
                           waits=[("f_mm", P.val("f_mm"))], inc="f_mm_g")
                w2_quant_slice(63)
            P.barrier("p2")

        def _ph_3():
            NSC = 8
            SCW = HID // NSC
            with contextlib.ExitStack() as ph:
                phase_reset()
                w2sb = [A16.take(8 * DM) for i in range(2)]
                hqfm = [A16.take(8 * DM) for i in range(2)]
                hpc = [A32.take(SCW) for i in range(2)]
                qA = A32.take(SCW)
                qB = A32.take(SCW)
                qI = AI.take(SCW)
                hqv = [A16.take(SCW) for i in range(2)]
                ev = [A32.take(DM) for i in range(1)]
                x1c = A32.take(DM)
                yac = A32.take(DM)
                yst = A32.take(DM)
                ps3 = [ph.enter_context(nc.psum_tensor(f"ps3{i}", [128, DM], F32)) for i in range(2)]

                def quant_hq(sc, tc):
                    i = sc * NT + tc
                    pb = i % 2
                    w = [("g_q", P.val("g_q"))] if i >= 2 else []
                    ld = P.dma("sync", hpc[pb][:],
                               h_hbm[tc * 128:(tc + 1) * 128, sc * SCW:(sc + 1) * SCW],
                               waits=w, inc="g_hld")
                    P.emit("vector", lambda e, pb=pb, tc=tc: e.tensor_scalar(
                        qA[:], hpc[pb][:], s3col[:, tc:tc + 1], None, ALU.mult),
                        waits=[("g_hld", ld), ("f_s3", P.val("f_s3"))])
                    P.emit("vector", lambda e: e.tensor_copy(qI[:], qA[:]))
                    P.emit("vector", lambda e: e.scalar_tensor_tensor(
                        qB[:], qA[:], 0.5, qI[:], ALU.subtract, ALU.is_ge))
                    wst = [("g_hqst", P.val("g_hqst"))] if i >= 2 else []
                    qd = P.emit("vector", lambda e, pb=pb: e.tensor_tensor(
                        hqv[pb][:], qI[:], qB[:], ALU.add),
                        waits=wst, inc="g_q")
                    P.dma("sync", hq_hbm[tc * 128:(tc + 1) * 128, sc * SCW:(sc + 1) * SCW],
                          hqv[pb][:], waits=[("g_q", qd)], inc="g_hqst")

                for tc in range(NT):
                    quant_hq(0, tc)

                for sc in range(NSC):
                    g_hq_need = P.val("g_hqst")

                    sb = sc % 2
                    wfree = [("g_w2free", P.val("g_w2free"))] if sc >= 2 else [("w2q_st", P.val("w2q_st"))]
                    for kc in range(8):
                        P.dma("gpsimd", w2sb[sb][:, kc * DM:(kc + 1) * DM],
                              w2q_hbm[sc * SCW + kc * 128: sc * SCW + (kc + 1) * 128, :],
                              waits=(wfree if kc == 0 else []), inc="g_w2ld")
                    whq = [("g_hqfree", P.val("g_hqfree"))] if sc >= 2 else []
                    for kc in range(8):
                        P.dma_t("sync", hqfm[sb][:, kc * DM:(kc + 1) * DM],
                                hq_hbm[:, sc * SCW + kc * 128: sc * SCW + (kc + 1) * 128],
                                waits=(whq if kc == 0 else []) +
                                      ([("g_hqst", g_hq_need)] if kc == 0 else []),
                                inc="g_hqld")
                    g_ld_need = P.val("g_w2ld")
                    g_hqld_need = P.val("g_hqld")

                    for tc in range(NT):
                        tb = tc % 2
                        wl = [("g_w2ld", g_ld_need), ("g_hqld", g_hqld_need)]
                        if P.val("g_ev") >= 2:
                            wl.append(("g_ev", P.val("g_ev") - 1))
                        first = True
                        for kc in range(8):
                            lhs = hqfm[sb][:, kc * DM + tc * 128: kc * DM + (tc + 1) * 128]
                            for nb in range(4):
                                rhs = w2sb[sb][:, kc * DM + nb * 512: kc * DM + (nb + 1) * 512]
                                gmm = P.emit("tensor", lambda e, l=lhs, r=rhs, nb=nb, kc=kc, tb=tb:
                                             e.matmul(ps3[tb][:, nb * 512:(nb + 1) * 512],
                                                      l, r, start=(kc == 0), stop=(kc == 7)),
                                             waits=(wl if first else []),
                                             inc=("g_mm" if (kc == 7 and nb == 3) else None))
                                first = False
                        if sc + 1 < NSC:
                            quant_hq(sc + 1, tc)
                        eb = 0
                        acsem = "g_ac%02d" % tc
                        wev = [("g_mm", P.val("g_mm"))]
                        if P.val("g_evf%d" % eb) > 0:
                            wev.append(("g_evf%d" % eb, P.val("g_evf%d" % eb)))
                        evd = P.emit("scalar", lambda e, tb=tb, eb=eb: e.copy(
                            ev[eb][:], ps3[tb][:]),
                            waits=wev, inc="g_ev")
                        if sc == 0:
                            P.emit("gpsimd", lambda e, tc=tc, eb=eb: e.dma_start(
                                out=yacc_hbm[tc * 128:(tc + 1) * 128, :], in_=ev[eb][:]),
                                waits=[("g_ev", evd)], inc=acsem, inc_n=16)
                        else:
                            P.emit("gpsimd", lambda e, tc=tc, eb=eb: e.dma_start(
                                out=yacc_hbm[tc * 128:(tc + 1) * 128, :], in_=ev[eb][:],
                                accum_op=ALU.add),
                                waits=[("g_ev", evd), (acsem, P.val(acsem))],
                                inc=acsem, inc_n=16)
                        P.emit("gpsimd", lambda e: e.nop(),
                               waits=[(acsem, P.val(acsem))], inc="g_evf%d" % eb)
                        if sc == NSC - 1:
                            wld = [("g_yst", P.val("g_yst"))] if tc >= 2 else []
                            P.dma("sync", yac[:], yacc_hbm[tc * 128:(tc + 1) * 128, :],
                                  waits=[("g_ac%02d" % tc, P.val("g_ac%02d" % tc))] + wld,
                                  inc="g_yld")
                            P.dma("sync", x1c[:], x1_hbm[tc * 128:(tc + 1) * 128, :],
                                  inc="g_yld")
                            fy = P.emit("vector", lambda e, tc=tc: e.scalar_tensor_tensor(
                                yst[:], yac[:], fac3[:, tc:tc + 1], x1c[:],
                                ALU.mult, ALU.add),
                                waits=[("g_yld", P.val("g_yld")),
                                       ("g_yst", P.val("g_yst"))], inc="g_y")
                            P.dma("sync", y_out[tc * 128:(tc + 1) * 128, :], yst[:],
                                  waits=[("g_y", fy)], inc="g_yst")
                    P.emit("tensor", lambda e: e.nop(),
                           waits=[("g_mm", P.val("g_mm"))], inc="g_w2free")
                    P.emit("tensor", lambda e: e.nop(),
                           waits=[("g_mm", P.val("g_mm"))], inc="g_hqfree")
            P.emit("sync", None, waits=[("g_yst", P.val("g_yst"))])

        if "1b" in phases:
            _ph_1b()
        if "1c" in phases:
            _ph_1c()
        if "2" in phases:
            _ph_2()
        if "3" in phases:
            _ph_3()

        with nc.Block() as block:
            P.replay(block)

    return nc



def _rope_tables():
    inv_freq = (1.0 / (THETA ** (np.arange(0, HD, 2, dtype=np.float64) / HD))
                ).astype(np.float32)
    pos = np.arange(S, dtype=np.float32)
    freqs = (pos[:, None] * inv_freq[None, :]).astype(np.float32)
    a = freqs.astype(np.float64)
    return np.cos(a).astype(np.float32), np.sin(a).astype(np.float32)


def _masks(hf):
    m = np.zeros((16, 128, GQ), dtype=np.float32)
    kp = np.arange(128)[:, None]
    qf = np.arange(GQ)[None, :]
    for kt in range(8):
        d = kt * 128 + kp - qf
        allowed = (d >= 1) & (d <= SW)
        m[kt] = np.where(allowed, 0.0, -3334.0)
        if hf == 0:
            al0 = allowed & (kt * 128 + kp >= HALO)
            m[8 + kt] = np.where(al0, 0.0, -3334.0)
        else:
            m[8 + kt] = m[kt]
    return m.astype(ml_dtypes.bfloat16)


def make_in_maps(L0_attn_hidden_states, L0_attn_Wq, L0_attn_Wk, L0_attn_Wv,
                 L0_attn_Wo, L0_ffn_w1, L0_ffn_w2):
    x = np.asarray(L0_attn_hidden_states, dtype=np.float32)
    wq = np.ascontiguousarray(np.asarray(L0_attn_Wq, dtype=np.float32).T)
    wk = np.ascontiguousarray(np.asarray(L0_attn_Wk, dtype=np.float32).T)
    wv = np.ascontiguousarray(np.asarray(L0_attn_Wv, dtype=np.float32).T)
    wo = np.ascontiguousarray(np.asarray(L0_attn_Wo, dtype=np.float32).T)
    w1 = np.ascontiguousarray(np.asarray(L0_ffn_w1, dtype=np.float32).T)
    w2 = np.ascontiguousarray(np.asarray(L0_ffn_w2, dtype=np.float32).T)
    cosT, sinT = _rope_tables()
    consts = np.tile(np.asarray(CONSTS, dtype=np.float32)[None, :], (128, 1))
    ones16 = np.ones((128, 128), dtype=ml_dtypes.bfloat16)
    in_maps = []
    for core in range(N_CORES):
        b, hf = core // 2, core % 2
        xc = np.zeros((C, DM), dtype=np.float32)
        lo = hf * T - HALO
        src_lo = max(lo, 0)
        xc[src_lo - lo:, :] = x[b, src_lo:hf * T + T, :]
        p0 = hf * T - HALO
        idx = np.clip(np.arange(p0, p0 + C), 0, S - 1)
        in_maps.append({
            "x_tm": np.ascontiguousarray(xc),
            "wqT": wq, "wkT": wk, "wvT": wv, "woT": wo,
            "w1T": w1, "w2T": w2,
            "ropec": np.ascontiguousarray(cosT[idx]),
            "ropes": np.ascontiguousarray(sinT[idx]),
            "masks": _masks(hf), "onesb": ones16, "consts": consts,
        })
    return in_maps


def kernel(L0_attn_hidden_states, L0_attn_Wq, L0_attn_Wk, L0_attn_Wv,
           L0_attn_Wo, L0_ffn_w1, L0_ffn_w2):
    in_maps = make_in_maps(L0_attn_hidden_states, L0_attn_Wq, L0_attn_Wk,
                           L0_attn_Wv, L0_attn_Wo, L0_ffn_w1, L0_ffn_w2)
    nc = build_program()
    res = run_bass_kernel_spmd(nc, in_maps, core_ids=list(range(N_CORES)))
    out = np.empty((B, S, DM), dtype=np.float32)
    for core in range(N_CORES):
        b, hf = core // 2, core % 2
        out[b, hf * T:(hf + 1) * T, :] = res.results[core]["y"]
    return out
